# version 28
# baseline (speedup 1.0000x reference)
import contextlib
import numpy as np
import ml_dtypes
import concourse.bass as bass
import concourse.mybir as mybir
from concourse.bass_utils import run_bass_kernel_spmd

F32 = mybir.dt.float32
BF16 = mybir.dt.bfloat16
AF = mybir.ActivationFunctionType
ALU = mybir.AluOpType
AX = mybir.AxisListType

B, S, D = 2, 8192, 1024
H, DK = 16, 64
FF = 4096
TOK = S // 4
NT = TOK // 128
TT = TOK // 512
EPS = 1e-5
NW = FF * D

_DEBUG_TAPS = {}


class Buf:
    __slots__ = ("w", "r", "name")

    def __init__(self, name=""):
        self.w = None
        self.r = []
        self.name = name


class Prog:
    ENGS = ("tensor", "vector", "scalar", "gpsimd", "sync")
    DMAQ = ("sync", "gpsimd", "scalar")
    R = 8

    def __init__(self):
        self.ops = []
        self.eng_ops = {e: [] for e in self.ENGS}
        self.dma_use = {}

    def op(self, eng, fn, reads=(), writes=(), dma=False, out=False):
        i = len(self.ops)
        deps = set()
        for b in reads:
            if b.w is not None:
                deps.add(b.w)
            b.r.append(i)
        for b in writes:
            if b.w is not None:
                deps.add(b.w)
            deps.update(b.r)
            b.w = i
            b.r = []
        deps.discard(i)
        o = dict(eng=eng, fn=fn, deps=deps, dma=dma, out=out, slot=None, use=None)
        if dma:
            assert eng in self.DMAQ, f"dma on engine {eng}"
            nq = sum(self.dma_use.get((eng, s), 0) for s in range(self.R))
            slot = nq % self.R
            use = self.dma_use.get((eng, slot), 0)
            self.dma_use[(eng, slot)] = use + 1
            o["slot"], o["use"] = slot, use
        self.ops.append(o)
        self.eng_ops[eng].append(i)
        return i

    def emit(self, block, sems, dsems):
        n = len(self.ops)
        signal = [False] * n
        for o in self.ops:
            for d in o["deps"]:
                od = self.ops[d]
                if od["eng"] != o["eng"] or o["dma"] or od["dma"] or od["eng"] != "tensor":
                    signal[d] = True
        for i, o in enumerate(self.ops):
            if o["out"] or o["dma"]:
                signal[i] = True
        semval = [0] * n
        cnt = {e: 0 for e in self.ENGS}
        for i, o in enumerate(self.ops):
            if o["dma"]:
                semval[i] = 16 * (o["use"] + 1)
            else:
                if signal[i]:
                    cnt[o["eng"]] += 1
                semval[i] = cnt[o["eng"]]
        self.final_cnt = cnt

        def wait_key(d):
            od = self.ops[d]
            if od["dma"]:
                return (od["eng"], od["slot"]), semval[d]
            return od["eng"], semval[d]

        def sem_of(key):
            return dsems[key] if isinstance(key, tuple) else sems[key]

        for eng in self.ENGS:
            idxs = self.eng_ops[eng]
            if not idxs:
                continue

            def body(engine, idxs=idxs, eng=eng):
                last_wait = {}
                for i in idxs:
                    o = self.ops[i]
                    waits = {}
                    for d in o["deps"]:
                        od = self.ops[d]
                        if (od["eng"] == eng == "tensor"
                                and not od["dma"] and not o["dma"]):
                            continue
                        k, v = wait_key(d)
                        if v > waits.get(k, 0):
                            waits[k] = v
                    if o["dma"] and o["use"] >= 1:
                        k = (eng, o["slot"])
                        v = 16 * o["use"]
                        if v > waits.get(k, 0):
                            waits[k] = v
                    for k in sorted(waits, key=str):
                        v = waits[k]
                        if last_wait.get(k, 0) >= v:
                            continue
                        last_wait[k] = v
                        engine.wait_ge(sem_of(k), v)
                    ins = o["fn"](engine)
                    if signal[i]:
                        if o["dma"]:
                            ins.then_inc(dsems[(eng, o["slot"])], 16)
                        else:
                            ins.then_inc(sems[eng], 1)

            getattr(block, eng)(body)

        @block.gpsimd
        def _(gpsimd):
            for e in self.ENGS:
                if self.final_cnt[e]:
                    gpsimd.wait_ge(sems[e], self.final_cnt[e])
            for (q, s), u in sorted(self.dma_use.items()):
                gpsimd.wait_ge(dsems[(q, s)], 16 * u)


def build_program():
    nc = bass.Bass("TRN2", target_bir_lowering=False)
    P = Prog()

    xb_d = nc.dram_tensor("x_batch", [S, D], F32, kind="ExternalInput").ap()
    xo_d = nc.dram_tensor("x_own", [TOK, D], F32, kind="ExternalInput").ap()
    wqT_d = nc.dram_tensor("wqT", [D, D], F32, kind="ExternalInput").ap()
    wkT_d = nc.dram_tensor("wkT", [D, D], F32, kind="ExternalInput").ap()
    wvT_d = nc.dram_tensor("wvT", [D, D], F32, kind="ExternalInput").ap()
    woT_d = nc.dram_tensor("woT", [D, D], F32, kind="ExternalInput").ap()
    w1T_d = nc.dram_tensor("w1T", [D, FF], F32, kind="ExternalInput").ap()
    w2T_d = nc.dram_tensor("w2T", [FF, D], F32, kind="ExternalInput").ap()
    b1_d = nc.dram_tensor("b1", [128, FF // 128], F32, kind="ExternalInput").ap()
    b2_d = nc.dram_tensor("b2", [1, D], F32, kind="ExternalInput").ap()
    dmask_d = nc.dram_tensor("dmask", [128, 32], BF16, kind="ExternalInput").ap()

    out_d = nc.dram_tensor("out", [TOK, D], F32, kind="ExternalOutput").ap()

    ln1b_d = nc.dram_tensor("ln1b", [S, D], BF16).ap()
    ln1o_d = nc.dram_tensor("ln1o", [TOK, D], BF16).ap()
    ln3_d = nc.dram_tensor("ln3", [TOK, D], BF16).ap()
    kfm_d = nc.dram_tensor("kfm", [D, S], BF16).ap()
    vtm_d = nc.dram_tensor("vtm", [S, H * 65], BF16).ap()
    wq1_d = nc.dram_tensor("wq1", [FF // 128, 128, 8, 128], BF16).ap()
    wq2_d = nc.dram_tensor("wq2", [FF, D], BF16).ap()

    dbg = {}
    for name, (shape, dt) in _DEBUG_TAPS.items():
        dbg[name] = nc.dram_tensor("dbg_" + name, shape, dt, kind="ExternalOutput").ap()

    es = contextlib.ExitStack()

    def sb(name, shape, dt=F32):
        return es.enter_context(nc.sbuf_tensor(name, shape, dt))

    with es:
        x_sb = sb("x_sb", [128, NT * 1024])
        qu_sb = sb("qu_sb", [128, 16384], BF16)
        fm_sb = sb("fm_sb", [128, 16 * 512], BF16)
        wA_sb = sb("wA_sb", [128, 8 * 1024], BF16)
        wB_sb = sb("wB_sb", [128, 8 * 1024], BF16)
        kband_sb = sb("kband_sb", [128, 2 * 1024], BF16)
        vband_sb = sb("vband_sb", [128, 2 * 1040], BF16)
        expT_sb = sb("expT_sb", [128, 2048], BF16)
        onorm_sb = sb("onorm_sb", [128, 8 * 512], BF16)
        ontmp_sb = sb("ontmp_sb", [64, 512], BF16)
        xb_sb = sb("xb_sb", [128, 2048])
        lno_sb = sb("lno_sb", [128, 2048], BF16)
        kevac_sb = sb("kevac_sb", [128, 1024], BF16)
        vstage_sb = sb("vstage_sb", [128, 2 * (H * 65)], BF16)
        qn_sb = vstage_sb
        w1t_sb = sb("w1t_sb", [128, 2 * 1024], BF16)
        ontmp_sb = w1t_sb
        w2t_sb = sb("w2t_sb", [128, 2 * 1024], BF16)
        b1_sb = sb("b1_sb", [128, FF // 128])
        b2bc_sb = sb("b2bc_sb", [128, D], BF16)
        dmask_sb = sb("dmask_sb", [128, 32], BF16)
        stats_sb = sb("stats_sb", [128, 24])
        aggr_sb = sb("aggr_sb", [128, 4])
        rstd_sb = sb("rstd_sb", [128, 2])
        eps_sb = sb("eps_sb", [128, 1])
        onesall_sb = sb("onesall_sb", [128, 128])
        zscr_sb = sb("zscr_sb", [128, 1024])
        onf_sb = zscr_sb
        qscr_sb = sb("qscr_sb", [128, 2 * 512])
        qp_sb = sb("qp_sb", [128, 2 * 512], BF16)
        asum_sb = sb("asum_sb", [128, 128])
        gam_sb = sb("gam_sb", [1, 64])
        ginv_sb = sb("ginv_sb", [128, 2])
        ffno_sb = xb_sb
        lntmp_sb = xb_sb

        sc_ps = es.enter_context(nc.psum_tensor("sc_ps", [128, 2048], F32))
        oa_ps = es.enter_context(nc.psum_tensor("oa_ps", [128, 1024], F32))
        op_ps = es.enter_context(nc.psum_tensor("op_ps", [128, 1024], F32))

        x_t = [Buf(f"x{t}") for t in range(NT)]
        qu = Buf("qu")
        fm = [Buf(f"fm{i}") for i in range(16)]
        wA, wB = Buf("wA"), Buf("wB")
        kband = [Buf("kb0"), Buf("kb1")]
        vband = [Buf("vb0"), Buf("vb1")]
        expT = [Buf(f"exp{i}") for i in range(4)]
        onorm = [Buf(f"on{pr}") for pr in range(8)]
        ontmp = None
        xb = [Buf("xb0"), Buf("xb1")]
        lno = [Buf("lno0"), Buf("lno1")]
        kevac = [Buf("ke0"), Buf("ke1")]
        vstage = [Buf("vs0"), Buf("vs1")]
        w1t = [Buf("w1t0"), Buf("w1t1")]
        w2t = [Buf("w2t0"), Buf("w2t1")]
        b1b, b2bc = Buf("b1"), Buf("b2bc")
        dmaskb = Buf("dmask")
        statsb = [Buf("st0"), Buf("st1")]
        aggrb = [Buf("ag0"), Buf("ag1")]
        rstdb = [Buf("rs0"), Buf("rs1")]
        epsb, onesb = Buf("eps"), Buf("ones")
        onf, zscr = Buf("onf"), Buf("zscr")
        qscr = [Buf(f"qs{i}") for i in range(2)]
        qp = [Buf(f"qp{i}") for i in range(2)]
        qn = vstage
        asum, gam, ginv = Buf("asum"), Buf("gam"), Buf("ginv")
        lntmpb = xb[0]
        sc = [Buf(f"sc{i}") for i in range(4)]
        oa = [Buf("oa0"), Buf("oa1")]
        op = Buf("op")
        ln1b_g = [Buf(f"l1b{g}") for g in range(16)]
        ln1o_g = [Buf(f"l1o{g}") for g in range(4)]
        ln3_g = [Buf(f"l3{g}") for g in range(4)]
        kfm, vtm = Buf("kfm_d"), Buf("vtm_d")
        wq1, wq2 = Buf("wq1_d"), Buf("wq2_d")

        O = P.op

        def tap(name, src_ap, reads):
            if name in dbg:
                O("gpsimd", lambda e, s=src_ap, d=dbg[name]: e.dma_start(out=d, in_=s),
                  reads=list(reads), dma=True, out=True)

        O("vector", lambda e: e.memset(eps_sb[:], EPS), writes=[epsb])
        O("vector", lambda e: e.memset(onesall_sb[:], 1.0), writes=[onesb])
        O("sync", lambda e: e.dma_start(out=dmask_sb[:], in_=dmask_d), writes=[dmaskb], dma=True)
        O("sync", lambda e: e.dma_start(out=b1_sb[:], in_=b1_d), writes=[b1b], dma=True)
        O("sync", lambda e: e.dma_start(out=xb_sb[0:1, 0:1024], in_=b2_d), writes=[xb[0]], dma=True)
        for vsl in range(2):
            O("vector", lambda e, vsl=vsl: e.memset(
                vstage_sb[:, vsl * (H * 65):(vsl + 1) * (H * 65)]
                .rearrange("p (h c) -> p h c", h=H)[:, :, 64:65], 1.0),
              writes=[vstage[vsl]])
        for eh in range(2):
            O("tensor", lambda e, eh=eh: e.matmul(
                op_ps[:, 0:512], onesall_sb[0:1, :],
                xb_sb[0:1, eh * 512:(eh + 1) * 512],
                start=True, stop=True), reads=[onesb, xb[0]], writes=[op])
            O("vector", lambda e, eh=eh: e.tensor_copy(
                b2bc_sb[:, eh * 512:(eh + 1) * 512], op_ps[:, 0:512]),
              reads=[op], writes=[b2bc])

        def ln_tile(x_ap, out_ap, x_bufs, out_bufs, sl, act_apply=False):
            st = stats_sb[:, sl * 12:(sl + 1) * 12]
            ag = aggr_sb[:, sl * 2:(sl + 1) * 2]
            rs = rstd_sb[:, sl:sl + 1]
            O("vector", lambda e: e.bn_stats(st[:, 0:6], x_ap[:, 0:512]),
              reads=list(x_bufs), writes=[statsb[sl]])
            O("vector", lambda e: e.bn_stats(st[:, 6:12], x_ap[:, 512:1024]),
              reads=list(x_bufs), writes=[statsb[sl]])
            O("vector", lambda e: e.bn_aggr(ag, st),
              reads=[statsb[sl]], writes=[aggrb[sl]])
            O("scalar", lambda e: e.activation(rs, ag[:, 1:2], AF.Ln, bias=eps_sb[:, 0:1]),
              reads=[aggrb[sl], epsb], writes=[rstdb[sl]])
            O("scalar", lambda e: e.activation(rs, rs, AF.Exp, scale=-0.5),
              reads=[rstdb[sl]], writes=[rstdb[sl]])
            O("vector", lambda e: e.tensor_scalar(out_ap, x_ap, ag[:, 0:1], rs,
                                                  ALU.subtract, ALU.mult),
              reads=list(x_bufs) + [aggrb[sl], rstdb[sl]], writes=list(out_bufs))

        def load_w_steps(wd_ap, dst_sb, dst_buf):
            steps = []
            for step in range(16):
                c, hh = step // 2, step % 2
                sl = step % 2

                def emit(c=c, hh=hh, sl=sl):
                    qv = qscr_sb[:, sl * 512:(sl + 1) * 512]
                    O("sync", lambda e, c=c, hh=hh, qv=qv: e.dma_start(
                        out=qv, in_=wd_ap[c * 128:(c + 1) * 128,
                                          hh * 512:(hh + 1) * 512]),
                      writes=[qscr[sl]], dma=True)
                    O("scalar", lambda e, c=c, hh=hh, qv=qv: e.copy(
                        dst_sb[:, c * 1024 + hh * 512: c * 1024 + (hh + 1) * 512], qv),
                      reads=[qscr[sl]], writes=[dst_buf])
                steps.append(emit)
            return steps

        def load_w_bf16(wd_ap, dst_sb, dst_buf):
            for s_ in load_w_steps(wd_ap, dst_sb, dst_buf):
                s_()

        wq_steps = load_w_steps(wqT_d, wB_sb, wB)
        for g4 in range(4):
            O("sync", lambda e, g4=g4: e.dma_start(
                out=x_sb[:, g4 * 4096:(g4 + 1) * 4096]
                .rearrange("p (t d) -> p t d", t=4),
                in_=xo_d[g4 * 512:(g4 + 1) * 512, :]
                .rearrange("(t p) d -> p t d", p=128)),
              writes=x_t[g4 * 4:(g4 + 1) * 4], dma=True)
            for _ in range(4):
                if wq_steps:
                    wq_steps.pop(0)()
        def ln1b_tile(t):
            if True:
                sl = t % 2
                xa = xb_sb[:, sl * 1024:(sl + 1) * 1024]
                la = lno_sb[:, sl * 1024:(sl + 1) * 1024]
                O("sync", lambda e, t=t, xa=xa: e.dma_start(
                    out=xa, in_=xb_d[t * 128:(t + 1) * 128, :]), writes=[xb[sl]], dma=True)
                ln_tile(xa, la, [xb[sl]], [lno[sl]], sl)
                O("gpsimd", lambda e, t=t, la=la: e.dma_start(
                    out=ln1b_d[t * 128:(t + 1) * 128, :], in_=la),
                  reads=[lno[sl]], writes=[ln1b_g[t // 4]], dma=True)

        def ln1b_group(tt):
            for i in range(4):
                ln1b_tile(4 * tt + i)

        def kv_transposes(tt):
            fs = (tt % 2) * 8
            for dc in range(8):
                O("sync", lambda e, tt=tt, dc=dc, fs=fs: e.dma_start_transpose(
                    fm_sb[:, (fs + dc) * 512:(fs + dc + 1) * 512],
                    ln1b_d[tt * 512:(tt + 1) * 512, dc * 128:(dc + 1) * 128]),
                  reads=[ln1b_g[tt]], writes=[fm[fs + dc]], dma=True)

        load_w_bf16(wqT_d, wB_sb, wB)

        def ln1o_tile(t):
            sl = t % 2
            xa = x_sb[:, t * 1024:(t + 1) * 1024]
            la = lno_sb[:, sl * 1024:(sl + 1) * 1024]
            ln_tile(xa, la, [x_t[t]], [lno[sl]], sl)
            O("gpsimd", lambda e, t=t, la=la: e.dma_start(
                out=ln1o_d[t * 128:(t + 1) * 128, :], in_=la),
              reads=[lno[sl]], writes=[ln1o_g[t // 4]], dma=True)

        def ln1o_group(tt):
            for i in range(4):
                ln1o_tile(4 * tt + i)

        def q_transposes(tt):
            fs = (tt % 2) * 8
            for dc in range(8):
                O("sync", lambda e, tt=tt, dc=dc, fs=fs: e.dma_start_transpose(
                    fm_sb[:, (fs + dc) * 512:(fs + dc + 1) * 512],
                    ln1o_d[tt * 512:(tt + 1) * 512, dc * 128:(dc + 1) * 128]),
                  reads=[ln1o_g[tt]], writes=[fm[fs + dc]], dma=True)

        wk_steps = load_w_steps(wkT_d, wA_sb, wA)
        ln1o_group(0)
        q_transposes(0)
        if TT > 1:
            ln1o_group(1)
            q_transposes(1)
        for tt in range(TT):
            fs = (tt % 2) * 8
            for qc in range(8):
                if qc % 2 == 0:
                    if tt + 2 < TT:
                        ln1o_tile(4 * (tt + 2) + qc // 2)
                    ln1b_tile(4 * tt + qc // 2)
                pslot = qc % 2
                pa = sc_ps[:, pslot * 512:(pslot + 1) * 512]
                for dc in range(8):
                    O("tensor", lambda e, qc=qc, dc=dc, fs=fs, pa=pa: e.matmul(
                        pa, wB_sb[:, dc * 1024 + qc * 128: dc * 1024 + (qc + 1) * 128],
                        fm_sb[:, (fs + dc) * 512:(fs + dc + 1) * 512],
                        start=(dc == 0), stop=(dc == 7)),
                      reads=[wB, fm[fs + dc]], writes=[sc[pslot]])
                O("scalar", lambda e, qc=qc, tt=tt, pa=pa: e.copy(
                    qu_sb[:, qc * 2048 + tt * 512: qc * 2048 + (tt + 1) * 512], pa),
                  reads=[sc[pslot]], writes=[qu])
            if tt + 2 < TT:
                q_transposes(tt + 2)
            for _ in range(4):
                if wk_steps:
                    wk_steps.pop(0)()
        while wk_steps:
            wk_steps.pop(0)()
        q_transposes(0)
        for tt in range(TT):
            fs = (tt % 2) * 8
            if tt + 1 < TT:
                q_transposes(tt + 1)
            for qc in range(8):
                pslot = qc % 2
                pa = sc_ps[:, pslot * 512:(pslot + 1) * 512]
                for dc in range(8):
                    O("tensor", lambda e, qc=qc, dc=dc, fs=fs, pa=pa: e.matmul(
                        pa, wB_sb[:, dc * 1024 + qc * 128: dc * 1024 + (qc + 1) * 128],
                        fm_sb[:, (fs + dc) * 512:(fs + dc + 1) * 512],
                        start=(dc == 0), stop=(dc == 7)),
                      reads=[wB, fm[fs + dc]], writes=[sc[pslot]])
                O("scalar", lambda e, qc=qc, tt=tt, pa=pa: e.copy(
                    qu_sb[:, qc * 2048 + tt * 512: qc * 2048 + (tt + 1) * 512], pa),
                  reads=[sc[pslot]], writes=[qu])

        quant_items = []
        NQ = 2

        def q_abs(c, cc, wd_ap, col, idx):
            sl = idx % NQ
            qv = qscr_sb[:, sl * 512:(sl + 1) * 512]

            def load():
                O("scalar", lambda e, qv=qv: e.dma_start(
                    out=qv, in_=wd_ap[c * 128:(c + 1) * 128, cc * 512:(cc + 1) * 512]),
                  writes=[qscr[sl]], dma=True)

            def compute():
                O("vector", lambda e, qv=qv: e.reduce_sum(
                    asum_sb[:, col:col + 1], qv, axis=AX.X, apply_absolute_value=True),
                  reads=[qscr[sl]], writes=[asum])
            return load, compute

        qidx = 0
        for c in range(8):
            for cc in range(8):
                quant_items.append(q_abs(c, cc, w1T_d, qidx, qidx)); qidx += 1
        for c in range(32):
            for cc in range(2):
                quant_items.append(q_abs(c, cc, w2T_d, qidx, qidx)); qidx += 1

        def q_gamma():
            O("tensor", lambda e: e.matmul(op_ps[0:1, 0:128], onesall_sb[:, 0:1],
                                           asum_sb[:, 0:128], start=True, stop=True),
              reads=[onesb, asum], writes=[op])
            O("vector", lambda e: e.reduce_sum(gam_sb[0:1, 0:1], op_ps[0:1, 0:64],
                                               axis=AX.X),
              reads=[op], writes=[gam])
            O("vector", lambda e: e.reduce_sum(gam_sb[0:1, 1:2], op_ps[0:1, 64:128],
                                               axis=AX.X),
              reads=[op], writes=[gam])
            O("vector", lambda e: e.tensor_single_scalar(
                gam_sb[0:1, 0:2], gam_sb[0:1, 0:2], float(NW) * EPS, ALU.add),
              reads=[gam], writes=[gam])
            O("vector", lambda e: e.reciprocal(gam_sb[0:1, 2:4], gam_sb[0:1, 0:2]),
              reads=[gam], writes=[gam])
            O("vector", lambda e: e.tensor_single_scalar(
                gam_sb[0:1, 2:4], gam_sb[0:1, 2:4], float(NW), ALU.mult),
              reads=[gam], writes=[gam])
            for w in range(2):
                O("tensor", lambda e, w=w: e.matmul(
                    op_ps[:, 128 + w:129 + w], onesall_sb[0:1, :],
                    gam_sb[0:1, 2 + w:3 + w], start=True, stop=True),
                  reads=[onesb, gam], writes=[op])
                O("vector", lambda e, w=w: e.tensor_copy(
                    ginv_sb[:, w:w + 1], op_ps[:, 128 + w:129 + w]),
                  reads=[op], writes=[ginv])

        quant_items.append((lambda: None, q_gamma))

        def q_quant_common(qv, pv, w, store_fn, sl):
            O("vector", lambda e, qv=qv, pv=pv: e.tensor_scalar(
                pv, qv, ginv_sb[:, w:w + 1], 0.5, ALU.mult, ALU.is_gt),
              reads=[qscr[sl], ginv], writes=[qp[sl]])
            nv = qn_sb[:, 0:512]
            O("vector", lambda e, qv=qv, nv=nv: e.tensor_scalar(
                nv, qv, ginv_sb[:, w:w + 1], -0.5, ALU.mult, ALU.is_lt),
              reads=[qscr[sl], ginv], writes=[qn[0]])
            O("vector", lambda e, pv=pv, nv=nv: e.tensor_sub(pv, pv, nv),
              reads=[qp[sl], qn[0]], writes=[qp[sl]])
            store_fn(pv, sl)

        def q_quant_w1(c, cc, idx):
            sl = idx % NQ
            qv = qscr_sb[:, sl * 512:(sl + 1) * 512]
            pv = qp_sb[:, sl * 512:(sl + 1) * 512]

            def load():
                O("scalar", lambda e, qv=qv: e.dma_start(
                    out=qv, in_=w1T_d[c * 128:(c + 1) * 128, cc * 512:(cc + 1) * 512]),
                  writes=[qscr[sl]], dma=True)

            def store(pv, sl):
                O("gpsimd", lambda e, pv=pv, c=c, cc=cc: e.dma_start(
                    out=wq1_d[cc * 4:(cc + 1) * 4, :, c, :].rearrange(
                        "fc p f -> p fc f"),
                    in_=pv.rearrange("p (fc f) -> p fc f", fc=4)),
                  reads=[qp[sl]], writes=[wq1], dma=True)

            return load, (lambda: q_quant_common(qv, pv, 0, store, sl))

        def q_quant_w2(c, cc, idx):
            sl = idx % NQ
            qv = qscr_sb[:, sl * 512:(sl + 1) * 512]
            pv = qp_sb[:, sl * 512:(sl + 1) * 512]

            def load():
                O("scalar", lambda e, qv=qv: e.dma_start(
                    out=qv, in_=w2T_d[c * 128:(c + 1) * 128, cc * 512:(cc + 1) * 512]),
                  writes=[qscr[sl]], dma=True)

            def store(pv, sl):
                O("gpsimd", lambda e, pv=pv, c=c, cc=cc: e.dma_start(
                    out=wq2_d[c * 128:(c + 1) * 128, cc * 512:(cc + 1) * 512], in_=pv),
                  reads=[qp[sl]], writes=[wq2], dma=True)

            return load, (lambda: q_quant_common(qv, pv, 1, store, sl))

        for c in range(8):
            for cc in range(8):
                quant_items.append(q_quant_w1(c, cc, qidx)); qidx += 1
        for c in range(32):
            for cc in range(2):
                quant_items.append(q_quant_w2(c, cc, qidx)); qidx += 1

        q_load_next = [0]
        q_comp_next = [0]
        QLEAD = NQ

        def quant_step(n):
            for _ in range(n):
                if (q_load_next[0] - q_comp_next[0] >= QLEAD
                        or (q_load_next[0] >= len(quant_items)
                            and q_comp_next[0] < len(quant_items))):
                    quant_items[q_comp_next[0]][1]()
                    q_comp_next[0] += 1
                if q_load_next[0] < len(quant_items):
                    quant_items[q_load_next[0]][0]()
                    q_load_next[0] += 1

        wv_steps = load_w_steps(wvT_d, wB_sb, wB)
        vs_ct = 0

        kv_transposes(0)
        kv_transposes(1)
        for tt in range(S // 512):
            fs = (tt % 2) * 8
            for kc in range(8):
                if wv_steps and kc % 2 == 1:
                    for _ in range(4):
                        if wv_steps:
                            wv_steps.pop(0)()
                if kc % 2 == 0 and tt + 4 < S // 512:
                    ln1b_tile(4 * (tt + 4) + kc // 2)
                pslot = kc % 4
                pa = sc_ps[:, pslot * 512:(pslot + 1) * 512]
                ke = kevac_sb[:, (kc % 2) * 512:(kc % 2 + 1) * 512]
                for dc in range(8):
                    O("tensor", lambda e, kc=kc, dc=dc, fs=fs, pa=pa: e.matmul(
                        pa, wA_sb[:, dc * 1024 + kc * 128: dc * 1024 + (kc + 1) * 128],
                        fm_sb[:, (fs + dc) * 512:(fs + dc + 1) * 512],
                        start=(dc == 0), stop=(dc == 7)),
                      reads=[wA, fm[fs + dc]], writes=[sc[pslot]])
                O("scalar", lambda e, ke=ke, pa=pa: e.copy(ke, pa),
                  reads=[sc[pslot]], writes=[kevac[kc % 2]])
                O("gpsimd", lambda e, kc=kc, tt=tt, ke=ke: e.dma_start(
                    out=kfm_d[kc * 128:(kc + 1) * 128, tt * 512:(tt + 1) * 512],
                    in_=ke),
                  reads=[kevac[kc % 2]], writes=[kfm], dma=True)
            for ts in range(4):
                vsl = vs_ct % 2
                vs_ct += 1
                va = vstage_sb[:, vsl * (H * 65):(vsl + 1) * (H * 65)]
                for vh in range(2):
                    pa = oa_ps[:, vh * 512:(vh + 1) * 512]
                    for dc in range(8):
                        O("tensor", lambda e, ts=ts, vh=vh, dc=dc, fs=fs, pa=pa: e.matmul(
                            pa, fm_sb[:, (fs + dc) * 512 + ts * 128:
                                      (fs + dc) * 512 + (ts + 1) * 128],
                            wB_sb[:, dc * 1024 + vh * 512: dc * 1024 + (vh + 1) * 512],
                            start=(dc == 0), stop=(dc == 7)),
                          reads=[wB, fm[fs + dc]], writes=[oa[vh]])
                    O("scalar", lambda e, vh=vh, va=va, pa=pa: e.copy(
                        va.rearrange("p (h c) -> p h c", h=H)[:, vh * 8:(vh + 1) * 8, 0:64],
                        pa.rearrange("p (h c) -> p h c", h=8)),
                      reads=[oa[vh]], writes=[vstage[vsl]])
                O("gpsimd", lambda e, tt=tt, ts=ts, va=va: e.dma_start(
                    out=vtm_d[tt * 512 + ts * 128: tt * 512 + (ts + 1) * 128, :],
                    in_=va[:, 0:H * 65] if False else va),
                  reads=[vstage[vsl]], writes=[vtm], dma=True)
            if tt + 2 < S // 512:
                kv_transposes(tt + 2)
        tap("ln1b", ln1b_d, ln1b_g)
        tap("kfm", kfm_d, [kfm])
        tap("vtm", vtm_d, [vtm])

        wo_steps = load_w_steps(woT_d, wA_sb, wA)

        ln3_q = []
        O("vector", lambda e: e.memset(op_ps[:, 0:1024], 0.0), writes=[op])
        gb_ct = 0

        def band_load(T, hp, b):
            nonlocal gb_ct
            sl = gb_ct % 2
            gb_ct += 1
            O("sync", lambda e, hp=hp, b=b, sl=sl: e.dma_start(
                out=kband_sb[:, sl * 1024:(sl + 1) * 1024],
                in_=kfm_d[hp * 128:(hp + 1) * 128, b * 1024:(b + 1) * 1024]),
              reads=[kfm], writes=[kband[sl]], dma=True)
            O("sync", lambda e, hp=hp, b=b, sl=sl: e.dma_start(
                out=vband_sb[:, sl * 1040:(sl + 1) * 1040]
                .rearrange("p (kt c) -> p kt c", kt=8),
                in_=vtm_d[b * 1024:(b + 1) * 1024, hp * 130:(hp + 1) * 130]
                .rearrange("(kt p) c -> p kt c", p=128)),
              reads=[vtm], writes=[vband[sl]], dma=True)
            return sl

        for T in range(TT):
            kvr = 2048 * (T + 1)
            nb = kvr // 1024
            nkb = kvr // 128
            blocks = [(hp, b) for hp in range(8) for b in range(nb)]
            slot_of = {}
            slot_of[blocks[0]] = band_load(T, *blocks[0])
            if len(blocks) > 1:
                slot_of[blocks[1]] = band_load(T, *blocks[1])
            pairs = [
                (sc_ps[:, 0:1024], [sc[0], sc[1]], expT_sb[:, 0:1024],
                 [expT[0], expT[1]]),
                (sc_ps[:, 1024:2048], [sc[2], sc[3]], expT_sb[:, 1024:2048],
                 [expT[2], expT[3]]),
                (op_ps[:, 0:1024], [op], kevac_sb[:, 0:1024],
                 [kevac[0], kevac[1]]),
            ]
            av_q = []
            uc = [0]
            epi_q = []

            def flush_av(n):
                while len(av_q) > n:
                    av_q.pop(0)()

            for bi, (hp, b) in enumerate(blocks):
                sl = slot_of.pop((hp, b))
                for g8 in range(8):
                    kb = 8 * b + g8
                    j = kb - 16 * T
                    qoff = 32 * j if j >= 0 else 0
                    pp, pbufs, ee, ebufs = pairs[uc[0] % 3]
                    uc[0] += 1
                    for h2 in range(2):
                        O("tensor", lambda e, h2=h2, hp=hp, kb=kb, T=T,
                          sl=sl, qoff=qoff, pp=pp: e.matmul(
                            pp[:, h2 * 512 + qoff:(h2 + 1) * 512],
                            kband_sb[64 * h2: 64 * h2 + 64,
                                     sl * 1024 + (kb % 8) * 128:
                                     sl * 1024 + (kb % 8 + 1) * 128],
                            qu_sb[64 * h2: 64 * h2 + 64,
                                  hp * 2048 + T * 512 + qoff:
                                  hp * 2048 + (T + 1) * 512],
                            start=True, stop=True),
                          reads=[kband[sl], qu], writes=pbufs)
                    flush_av(2)
                    if j >= 3:
                        for h2 in range(2):
                            O("scalar", lambda e, h2=h2, qoff=qoff, pp=pp,
                              ee=ee: e.activation(
                                ee[:, h2 * 512 + qoff:(h2 + 1) * 512],
                                pp[:, h2 * 512 + qoff:(h2 + 1) * 512],
                                AF.Exp, scale=0.125),
                              reads=pbufs, writes=ebufs)
                    else:
                        O("scalar", lambda e, pp=pp, ee=ee: e.activation(
                            ee, pp, AF.Exp, scale=0.125),
                          reads=pbufs, writes=ebufs)
                    if j >= 0:
                        for h2 in range(2):
                            O("gpsimd", lambda e, h2=h2, qoff=qoff, ee=ee: e.tensor_mul(
                                ee[:, h2 * 512 + qoff: h2 * 512 + qoff + 32],
                                ee[:, h2 * 512 + qoff: h2 * 512 + qoff + 32],
                                dmask_sb[:]),
                              reads=ebufs + [dmaskb], writes=ebufs)
                    def make_av(kb=kb, sl=sl, qoff=qoff, nkb=nkb, ee=ee,
                                ebufs=ebufs):
                        def emit_av():
                            for h2 in range(2):
                                O("tensor", lambda e, h2=h2, kb=kb, sl=sl,
                                  qoff=qoff, st=(kb == 0), ee=ee,
                                  sp=(kb == nkb - 1): e.matmul(
                                    oa_ps[0:65, h2 * 512 + qoff:(h2 + 1) * 512],
                                    vband_sb[:, sl * 1040 + (kb % 8) * 130 + h2 * 65:
                                             sl * 1040 + (kb % 8) * 130 + (h2 + 1) * 65],
                                    ee[:, h2 * 512 + qoff:(h2 + 1) * 512],
                                    start=st, stop=sp),
                                  reads=[vband[sl]] + ebufs, writes=[oa[h2]])
                        return emit_av

                    av_q.append(make_av())
                if bi + 2 < len(blocks):
                    flush_av(0)
                    slot_of[blocks[bi + 2]] = band_load(T, *blocks[bi + 2])
                if epi_q:
                    epi_q.pop(0)()
                if wo_steps:
                    wo_steps.pop(0)()
                    if wo_steps:
                        wo_steps.pop(0)()
                else:
                    quant_step(4)
                if ln3_q:
                    ln3_q.pop(0)()

                if b == nb - 1:
                    flush_av(0)
                    O("vector", lambda e: e.tensor_copy(
                        zscr_sb[0:65, 0:1024], oa_ps[0:65, 0:1024]),
                      reads=[oa[0], oa[1]], writes=[zscr, onf])

                    def make_epi(hp=hp):
                        def emit_epi():
                            O("scalar", lambda e: e.activation(
                                zscr_sb[64:65, 0:1024], zscr_sb[64:65, 0:1024],
                                AF.Ln),
                              reads=[zscr], writes=[zscr])
                            O("scalar", lambda e: e.activation(
                                zscr_sb[64:65, 0:1024], zscr_sb[64:65, 0:1024],
                                AF.Exp, scale=-1.0),
                              reads=[zscr], writes=[zscr])
                            for h2 in range(2):
                                hs = h2 * 512
                                O("tensor", lambda e, hs=hs: e.matmul(
                                    op_ps[0:64, hs:hs + 512],
                                    onesall_sb[64:65, 0:64],
                                    zscr_sb[64:65, hs:hs + 512],
                                    start=True, stop=True),
                                  reads=[onesb, zscr], writes=[op])
                            O("vector", lambda e, hp=hp: e.tensor_mul(
                                onorm_sb[0:64, hp * 512:(hp + 1) * 512],
                                onf_sb[0:64, 0:512], op_ps[0:64, 0:512]),
                              reads=[onf, op], writes=[onorm[hp]])
                            O("vector", lambda e: e.tensor_mul(
                                ontmp_sb[0:64, 0:512], onf_sb[0:64, 512:1024],
                                op_ps[0:64, 512:1024]),
                              reads=[onf, op], writes=[w1t[0]])
                            O("gpsimd", lambda e, hp=hp: e.dma_start(
                                out=onorm_sb[64:128, hp * 512:(hp + 1) * 512],
                                in_=ontmp_sb[0:64, 0:512]),
                              reads=[w1t[0]], writes=[onorm[hp]], dma=True)
                        return emit_epi

                    epi_q.append(make_epi())
            while epi_q:
                epi_q.pop(0)()
            for qs in range(4):
                xt = 4 * T + qs
                xa = x_sb[:, xt * 1024:(xt + 1) * 1024]
                if qs % 2 == 0:
                    pfull, pwb = op_ps[:, 0:1024], [op]
                else:
                    pfull, pwb = sc_ps[:, 0:1024], [sc[0], sc[1]]
                for pr in range(8):
                    for eh in range(2):
                        O("tensor", lambda e, qs=qs, eh=eh, pr=pr, pfull=pfull: e.matmul(
                            pfull[:, eh * 512:(eh + 1) * 512],
                            onorm_sb[:, pr * 512 + qs * 128: pr * 512 + (qs + 1) * 128],
                            wA_sb[:, pr * 1024 + eh * 512: pr * 1024 + (eh + 1) * 512],
                            start=(pr == 0), stop=(pr == 7)),
                          reads=[onorm[pr], wA], writes=pwb)
                sl = qs % 2
                st = stats_sb[:, sl * 12:(sl + 1) * 12]
                ag = aggr_sb[:, sl * 2:(sl + 1) * 2]
                rs = rstd_sb[:, sl:sl + 1]
                O("vector", lambda e, pfull=pfull, st=st: e.bn_stats(
                    st[:, 0:6], pfull[:, 0:512]), reads=pwb, writes=[statsb[sl]])
                O("vector", lambda e, pfull=pfull, st=st: e.bn_stats(
                    st[:, 6:12], pfull[:, 512:1024]), reads=pwb, writes=[statsb[sl]])
                O("vector", lambda e, st=st, ag=ag: e.bn_aggr(ag, st),
                  reads=[statsb[sl]], writes=[aggrb[sl]])
                O("scalar", lambda e, ag=ag, rs=rs: e.activation(
                    rs, ag[:, 1:2], AF.Ln, bias=eps_sb[:, 0:1]),
                  reads=[aggrb[sl], epsb], writes=[rstdb[sl]])
                O("scalar", lambda e, rs=rs: e.activation(rs, rs, AF.Exp, scale=-0.5),
                  reads=[rstdb[sl]], writes=[rstdb[sl]])
                O("vector", lambda e, pfull=pfull, ag=ag, rs=rs: e.tensor_scalar(
                    lntmp_sb[:, 0:1024], pfull, ag[:, 0:1], rs, ALU.subtract, ALU.mult),
                  reads=pwb + [aggrb[sl], rstdb[sl]], writes=[lntmpb])
                O("gpsimd", lambda e, xa=xa: e.tensor_add(xa, xa, lntmp_sb[:, 0:1024]),
                  reads=[x_t[xt], lntmpb], writes=[x_t[xt]])
                def make_ln3(xt=xt, sl=sl, T=T):
                    def emit_ln3():
                        xa2 = x_sb[:, xt * 1024:(xt + 1) * 1024]
                        la = lno_sb[:, sl * 1024:(sl + 1) * 1024]
                        ln_tile(xa2, la, [x_t[xt]], [lno[sl]], sl)
                        O("gpsimd", lambda e, xt=xt, la=la: e.dma_start(
                            out=ln3_d[xt * 128:(xt + 1) * 128, :], in_=la),
                          reads=[lno[sl]], writes=[ln3_g[T]], dma=True)
                    return emit_ln3

                ln3_q.append(make_ln3())
        while ln3_q:
            ln3_q.pop(0)()
        tap("ln3", ln3_d, ln3_g)
        while q_comp_next[0] < len(quant_items):
            quant_step(4)
        tap("wq1", wq1_d, [wq1])
        tap("wq2", wq2_d, [wq2])

        def ffn_transposes(tt):
            fs = (tt % 2) * 8
            for dc in range(8):
                O("sync", lambda e, tt=tt, dc=dc, fs=fs: e.dma_start_transpose(
                    fm_sb[:, (fs + dc) * 512:(fs + dc + 1) * 512],
                    ln3_d[tt * 512:(tt + 1) * 512, dc * 128:(dc + 1) * 128]),
                  reads=[ln3_g[tt]], writes=[fm[fs + dc]], dma=True)

        ffn_transposes(0)
        ffn_q = []
        w1s = [(w1t_sb[:, 0:1024], w1t[0]), (w1t_sb[:, 1024:2048], w1t[1]),
               (kband_sb[:, 0:1024], kband[0]), (kband_sb[:, 1024:2048], kband[1])]
        w2s = [(w2t_sb[:, 0:1024], w2t[0]), (w2t_sb[:, 1024:2048], w2t[1]),
               (vband_sb[:, 0:1024], vstage[0]), (vband_sb[:, 1040:2064], vstage[1])]
        for tt in range(TT):
            fs = (tt % 2) * 8
            if tt + 1 < TT:
                ffn_transposes(tt + 1)
            for fc in range(32):
                if fc == 0:
                    while len(ffn_q) > 2:
                        ffn_q.pop(0)()
                elif fc in (8, 16) and ffn_q:
                    ffn_q.pop(0)()
                wv, wb = w1s[fc % 4]
                O("sync", lambda e, fc=fc, wv=wv: e.dma_start(
                    out=wv, in_=wq1_d[fc, :, :, :].rearrange("p c f -> p (c f)")),
                  reads=[wq1], writes=[wb], dma=True)
                ua = oa_ps[:, (fc % 2) * 512:(fc % 2 + 1) * 512]
                for dc in range(8):
                    O("tensor", lambda e, fc=fc, dc=dc, fs=fs, wv=wv, ua=ua: e.matmul(
                        ua, wv[:, dc * 128:(dc + 1) * 128],
                        fm_sb[:, (fs + dc) * 512:(fs + dc + 1) * 512],
                        start=(dc == 0), stop=(dc == 7)),
                      reads=[wb, fm[fs + dc]], writes=[oa[fc % 2]])
                O("vector", lambda e, fc=fc, ua=ua: e.tensor_scalar(
                    qu_sb[:, fc * 512:(fc + 1) * 512], ua,
                    b1_sb[:, fc:fc + 1], 0.0, ALU.add, ALU.max),
                  reads=[oa[fc % 2], b1b], writes=[qu])
            pmap = {(0, 0): (sc_ps[:, 0:512], sc[0]), (0, 1): (sc_ps[:, 512:1024], sc[1]),
                    (0, 2): (sc_ps[:, 1024:1536], sc[2]), (0, 3): (sc_ps[:, 1536:2048], sc[3]),
                    (1, 0): (oa_ps[:, 0:512], oa[0]), (1, 1): (oa_ps[:, 512:1024], oa[1]),
                    (1, 2): (op_ps[:, 0:512], op), (1, 3): (op_ps[:, 512:1024], op)}
            for fc in range(32):
                wv, wb = w2s[fc % 4]
                O("sync", lambda e, fc=fc, wv=wv: e.dma_start(
                    out=wv, in_=wq2_d[fc * 128:(fc + 1) * 128, :]),
                  reads=[wq2], writes=[wb], dma=True)
                for ts in range(4):
                    for eh in range(2):
                        pa, pb = pmap[(eh, ts)]
                        O("tensor", lambda e, fc=fc, eh=eh, ts=ts, pa=pa, wv=wv: e.matmul(
                            pa, qu_sb[:, fc * 512 + ts * 128: fc * 512 + (ts + 1) * 128],
                            wv[:, eh * 512:(eh + 1) * 512],
                            start=(fc == 0), stop=(fc == 31)),
                          reads=[qu, wb], writes=[pb])
            for ts in range(4):
                def make_ln4(tt=tt, ts=ts):
                    def emit_ln4():
                        xt = 4 * tt + ts
                        xa = x_sb[:, xt * 1024:(xt + 1) * 1024]
                        ff = ffno_sb[:, 1024:2048]
                        for eh in (1, 0):
                            pa, pb = pmap[(eh, ts)]
                            O("vector", lambda e, pa=pa, eh=eh, ff=ff: e.tensor_add(
                                ff[:, eh * 512:(eh + 1) * 512], pa,
                                b2bc_sb[:, eh * 512:(eh + 1) * 512]),
                              reads=[pb, b2bc], writes=[xb[1]])
                        sl = ts % 2
                        st = stats_sb[:, sl * 12:(sl + 1) * 12]
                        ag = aggr_sb[:, sl * 2:(sl + 1) * 2]
                        rs = rstd_sb[:, sl:sl + 1]
                        O("vector", lambda e, ff=ff, st=st: e.bn_stats(
                            st[:, 0:6], ff[:, 0:512]),
                          reads=[xb[1]], writes=[statsb[sl]])
                        O("vector", lambda e, ff=ff, st=st: e.bn_stats(
                            st[:, 6:12], ff[:, 512:1024]),
                          reads=[xb[1]], writes=[statsb[sl]])
                        O("vector", lambda e, st=st, ag=ag: e.bn_aggr(ag, st),
                          reads=[statsb[sl]], writes=[aggrb[sl]])
                        O("scalar", lambda e, ag=ag, rs=rs: e.activation(
                            rs, ag[:, 1:2], AF.Ln, bias=eps_sb[:, 0:1]),
                          reads=[aggrb[sl], epsb], writes=[rstdb[sl]])
                        O("scalar", lambda e, rs=rs: e.activation(
                            rs, rs, AF.Exp, scale=-0.5),
                          reads=[rstdb[sl]], writes=[rstdb[sl]])
                        O("vector", lambda e, ff=ff, ag=ag, rs=rs: e.tensor_scalar(
                            lntmp_sb[:, 0:1024], ff, ag[:, 0:1], rs, ALU.subtract, ALU.mult),
                          reads=[xb[1], aggrb[sl], rstdb[sl]], writes=[lntmpb])
                        O("gpsimd", lambda e, xa=xa: e.tensor_add(xa, xa, lntmp_sb[:, 0:1024]),
                          reads=[x_t[xt], lntmpb], writes=[x_t[xt]])
                    return emit_ln4

                ffn_q.append(make_ln4())
        while ffn_q:
            ffn_q.pop(0)()

        O("sync", lambda e: e.dma_start(
            out=out_d.rearrange("(t p) d -> p t d", p=128),
            in_=x_sb[:].rearrange("p (t d) -> p t d", t=NT)),
          reads=x_t, dma=True, out=True)

        with contextlib.ExitStack() as es2:
            sems = {e: es2.enter_context(nc.semaphore("s_" + e)) for e in Prog.ENGS}
            dsems = {}
            for q in Prog.DMAQ:
                for s in range(Prog.R):
                    dsems[(q, s)] = es2.enter_context(nc.semaphore(f"d_{q}{s}"))
            with nc.Block() as block:
                P.emit(block, sems, dsems)

    return nc


def _stage_inputs(inputs):
    x = np.asarray(inputs["L0_attn_x"], dtype=np.float32)
    wq = np.asarray(inputs["L0_attn_q_heads"], dtype=np.float32)
    wk = np.asarray(inputs["L0_attn_k_heads"], dtype=np.float32)
    wv = np.asarray(inputs["L0_attn_v_heads"], dtype=np.float32)
    wo = np.asarray(inputs["L0_attn_output_proj"], dtype=np.float32)
    w1 = np.asarray(inputs["L0_ffn_W1"], dtype=np.float32)
    b1 = np.asarray(inputs["L0_ffn_b1"], dtype=np.float32)
    w2 = np.asarray(inputs["L0_ffn_W2"], dtype=np.float32)
    b2 = np.asarray(inputs["L0_ffn_b2"], dtype=np.float32)

    wqT = np.ascontiguousarray(wq.T)
    wkT = np.ascontiguousarray(wk.T)
    wvT = np.ascontiguousarray(wv.T)
    woT = np.ascontiguousarray(wo.T)
    w1T = np.ascontiguousarray(w1.T)
    w2T = np.ascontiguousarray(w2.T)
    b2r = np.ascontiguousarray(b2.reshape(1, D))
    b1r = np.ascontiguousarray(b1.reshape(FF // 128, 128).T)

    in_maps = []
    for c in range(8):
        b = c // 4
        r = c % 4
        kk, ttau = np.meshgrid(np.arange(128), np.arange(32), indexing="ij")
        dmask = (kk <= 4 * ttau + r).astype(ml_dtypes.bfloat16)
        xb = np.ascontiguousarray(x[b])
        xo = np.ascontiguousarray(x[b, r::4])
        in_maps.append({
            "x_batch": xb, "x_own": xo,
            "wqT": wqT, "wkT": wkT, "wvT": wvT, "woT": woT,
            "w1T": w1T, "w2T": w2T, "b1": b1r, "b2": b2r,
            "dmask": np.ascontiguousarray(dmask),
        })
    return in_maps


def kernel(**inputs):
    in_maps = _stage_inputs(inputs)
    nc = build_program()
    res = run_bass_kernel_spmd(nc, in_maps, core_ids=list(range(8)))
    out = np.empty((B, S, D), dtype=np.float32)
    for c in range(8):
        b, r = c // 4, c % 4
        out[b, r::4, :] = res.results[c]["out"]
    return out


# version 29
# speedup vs baseline: 1.1057x; 1.1057x over previous
import contextlib
import numpy as np
import ml_dtypes
import concourse.bass as bass
import concourse.mybir as mybir
from concourse.bass_utils import run_bass_kernel_spmd

F32 = mybir.dt.float32
BF16 = mybir.dt.bfloat16
AF = mybir.ActivationFunctionType
ALU = mybir.AluOpType
AX = mybir.AxisListType

B, S, D = 2, 8192, 1024
H, DK = 16, 64
FF = 4096
TOK = S // 4
NT = TOK // 128
TT = TOK // 512
EPS = 1e-5
NW = FF * D

_DEBUG_TAPS = {}


class Buf:
    __slots__ = ("w", "r", "name")

    def __init__(self, name=""):
        self.w = None
        self.r = []
        self.name = name


class Prog:
    ENGS = ("tensor", "vector", "scalar", "gpsimd", "sync")
    DMAQ = ("sync", "gpsimd", "scalar")
    R = 8

    def __init__(self):
        self.ops = []
        self.eng_ops = {e: [] for e in self.ENGS}
        self.dma_use = {}

    def op(self, eng, fn, reads=(), writes=(), dma=False, out=False):
        i = len(self.ops)
        deps = set()
        for b in reads:
            if b.w is not None:
                deps.add(b.w)
            b.r.append(i)
        for b in writes:
            if b.w is not None:
                deps.add(b.w)
            deps.update(b.r)
            b.w = i
            b.r = []
        deps.discard(i)
        o = dict(eng=eng, fn=fn, deps=deps, dma=dma, out=out, slot=None, use=None)
        if dma:
            assert eng in self.DMAQ, f"dma on engine {eng}"
            nq = sum(self.dma_use.get((eng, s), 0) for s in range(self.R))
            slot = nq % self.R
            use = self.dma_use.get((eng, slot), 0)
            self.dma_use[(eng, slot)] = use + 1
            o["slot"], o["use"] = slot, use
        self.ops.append(o)
        self.eng_ops[eng].append(i)
        return i

    def emit(self, block, sems, dsems):
        n = len(self.ops)
        signal = [False] * n
        for o in self.ops:
            for d in o["deps"]:
                od = self.ops[d]
                if od["eng"] != o["eng"] or o["dma"] or od["dma"] or od["eng"] != "tensor":
                    signal[d] = True
        for i, o in enumerate(self.ops):
            if o["out"] or o["dma"]:
                signal[i] = True
        semval = [0] * n
        cnt = {e: 0 for e in self.ENGS}
        for i, o in enumerate(self.ops):
            if o["dma"]:
                semval[i] = 16 * (o["use"] + 1)
            else:
                if signal[i]:
                    cnt[o["eng"]] += 1
                semval[i] = cnt[o["eng"]]
        self.final_cnt = cnt

        def wait_key(d):
            od = self.ops[d]
            if od["dma"]:
                return (od["eng"], od["slot"]), semval[d]
            return od["eng"], semval[d]

        def sem_of(key):
            return dsems[key] if isinstance(key, tuple) else sems[key]

        for eng in self.ENGS:
            idxs = self.eng_ops[eng]
            if not idxs:
                continue

            def body(engine, idxs=idxs, eng=eng):
                last_wait = {}
                for i in idxs:
                    o = self.ops[i]
                    waits = {}
                    for d in o["deps"]:
                        od = self.ops[d]
                        if (od["eng"] == eng == "tensor"
                                and not od["dma"] and not o["dma"]):
                            continue
                        k, v = wait_key(d)
                        if v > waits.get(k, 0):
                            waits[k] = v
                    if o["dma"] and o["use"] >= 1:
                        k = (eng, o["slot"])
                        v = 16 * o["use"]
                        if v > waits.get(k, 0):
                            waits[k] = v
                    for k in sorted(waits, key=str):
                        v = waits[k]
                        if last_wait.get(k, 0) >= v:
                            continue
                        last_wait[k] = v
                        engine.wait_ge(sem_of(k), v)
                    ins = o["fn"](engine)
                    if signal[i]:
                        if o["dma"]:
                            ins.then_inc(dsems[(eng, o["slot"])], 16)
                        else:
                            ins.then_inc(sems[eng], 1)

            getattr(block, eng)(body)

        @block.gpsimd
        def _(gpsimd):
            for e in self.ENGS:
                if self.final_cnt[e]:
                    gpsimd.wait_ge(sems[e], self.final_cnt[e])
            for (q, s), u in sorted(self.dma_use.items()):
                gpsimd.wait_ge(dsems[(q, s)], 16 * u)


def build_program():
    nc = bass.Bass("TRN2", target_bir_lowering=False)
    P = Prog()

    xb_d = nc.dram_tensor("x_batch", [S, D], F32, kind="ExternalInput").ap()
    xo_d = nc.dram_tensor("x_own", [TOK, D], F32, kind="ExternalInput").ap()
    wqT_d = nc.dram_tensor("wqT", [D, D], F32, kind="ExternalInput").ap()
    wkT_d = nc.dram_tensor("wkT", [D, D], F32, kind="ExternalInput").ap()
    wvT_d = nc.dram_tensor("wvT", [D, D], F32, kind="ExternalInput").ap()
    woT_d = nc.dram_tensor("woT", [D, D], F32, kind="ExternalInput").ap()
    w1T_d = nc.dram_tensor("w1T", [D, FF], F32, kind="ExternalInput").ap()
    w2T_d = nc.dram_tensor("w2T", [FF, D], F32, kind="ExternalInput").ap()
    b1_d = nc.dram_tensor("b1", [128, FF // 128], F32, kind="ExternalInput").ap()
    b2_d = nc.dram_tensor("b2", [1, D], F32, kind="ExternalInput").ap()
    dmask_d = nc.dram_tensor("dmask", [128, 32], BF16, kind="ExternalInput").ap()

    out_d = nc.dram_tensor("out", [TOK, D], F32, kind="ExternalOutput").ap()

    ln1b_d = nc.dram_tensor("ln1b", [S, D], BF16).ap()
    ln1o_d = nc.dram_tensor("ln1o", [TOK, D], BF16).ap()
    ln3_d = nc.dram_tensor("ln3", [TOK, D], BF16).ap()
    kfm_d = nc.dram_tensor("kfm", [D, S], BF16).ap()
    vtm_d = nc.dram_tensor("vtm", [S, H * 65], BF16).ap()
    wq1_d = nc.dram_tensor("wq1", [FF // 128, 128, 8, 128], BF16).ap()
    wq2_d = nc.dram_tensor("wq2", [FF, D], BF16).ap()

    dbg = {}
    for name, (shape, dt) in _DEBUG_TAPS.items():
        dbg[name] = nc.dram_tensor("dbg_" + name, shape, dt, kind="ExternalOutput").ap()

    es = contextlib.ExitStack()

    def sb(name, shape, dt=F32):
        return es.enter_context(nc.sbuf_tensor(name, shape, dt))

    with es:
        x_sb = sb("x_sb", [128, NT * 1024])
        qu_sb = sb("qu_sb", [128, 16384], BF16)
        fm_sb = sb("fm_sb", [128, 16 * 512], BF16)
        wA_sb = sb("wA_sb", [128, 8 * 1024], BF16)
        wB_sb = sb("wB_sb", [128, 8 * 1024], BF16)
        kband_sb = sb("kband_sb", [128, 2 * 1024], BF16)
        vband_sb = sb("vband_sb", [128, 2 * 1040], BF16)
        expT_sb = sb("expT_sb", [128, 2048], BF16)
        onorm_sb = sb("onorm_sb", [128, 8 * 512], BF16)
        ontmp_sb = sb("ontmp_sb", [64, 512], BF16)
        xb_sb = sb("xb_sb", [128, 2048])
        lno_sb = sb("lno_sb", [128, 2048], BF16)
        kevac_sb = sb("kevac_sb", [128, 1024], BF16)
        vstage_sb = sb("vstage_sb", [128, 2 * (H * 65)], BF16)
        qn_sb = vstage_sb
        w1t_sb = sb("w1t_sb", [128, 2 * 1024], BF16)
        ontmp_sb = w1t_sb
        w2t_sb = sb("w2t_sb", [128, 2 * 1024], BF16)
        b1_sb = sb("b1_sb", [128, FF // 128])
        b2bc_sb = sb("b2bc_sb", [128, D], BF16)
        dmask_sb = sb("dmask_sb", [128, 32], BF16)
        stats_sb = sb("stats_sb", [128, 24])
        aggr_sb = sb("aggr_sb", [128, 4])
        rstd_sb = sb("rstd_sb", [128, 2])
        eps_sb = sb("eps_sb", [128, 1])
        onesall_sb = sb("onesall_sb", [128, 128])
        zscr_sb = sb("zscr_sb", [128, 1024])
        onf_sb = zscr_sb
        qscr_sb = sb("qscr_sb", [128, 2 * 512])
        qp_sb = sb("qp_sb", [128, 2 * 512], BF16)
        asum_sb = sb("asum_sb", [128, 128])
        gam_sb = sb("gam_sb", [1, 64])
        ginv_sb = sb("ginv_sb", [128, 2])
        ffno_sb = xb_sb
        lntmp_sb = xb_sb

        sc_ps = es.enter_context(nc.psum_tensor("sc_ps", [128, 2048], F32))
        oa_ps = es.enter_context(nc.psum_tensor("oa_ps", [128, 1024], F32))
        op_ps = es.enter_context(nc.psum_tensor("op_ps", [128, 1024], F32))

        x_t = [Buf(f"x{t}") for t in range(NT)]
        qu = Buf("qu")
        fm = [Buf(f"fm{i}") for i in range(16)]
        wA, wB = Buf("wA"), Buf("wB")
        kband = [Buf("kb0"), Buf("kb1")]
        vband = [Buf("vb0"), Buf("vb1")]
        expT = [Buf(f"exp{i}") for i in range(4)]
        onorm = [Buf(f"on{pr}") for pr in range(8)]
        ontmp = None
        xb = [Buf("xb0"), Buf("xb1")]
        lno = [Buf("lno0"), Buf("lno1")]
        kevac = [Buf("ke0"), Buf("ke1")]
        vstage = [Buf("vs0"), Buf("vs1")]
        w1t = [Buf("w1t0"), Buf("w1t1")]
        w2t = [Buf("w2t0"), Buf("w2t1")]
        b1b, b2bc = Buf("b1"), Buf("b2bc")
        dmaskb = Buf("dmask")
        statsb = [Buf("st0"), Buf("st1")]
        aggrb = [Buf("ag0"), Buf("ag1")]
        rstdb = [Buf("rs0"), Buf("rs1")]
        epsb, onesb = Buf("eps"), Buf("ones")
        onf, zscr = Buf("onf"), Buf("zscr")
        qscr = [Buf(f"qs{i}") for i in range(2)]
        qp = [Buf(f"qp{i}") for i in range(2)]
        qn = vstage
        asum, gam, ginv = Buf("asum"), Buf("gam"), Buf("ginv")
        lntmpb = xb[0]
        sc = [Buf(f"sc{i}") for i in range(4)]
        oa = [Buf("oa0"), Buf("oa1")]
        op = Buf("op")
        ln1b_g = [Buf(f"l1b{g}") for g in range(16)]
        ln1o_g = [Buf(f"l1o{g}") for g in range(4)]
        ln3_g = [Buf(f"l3{g}") for g in range(4)]
        kfm, vtm = Buf("kfm_d"), Buf("vtm_d")
        wq1, wq2 = Buf("wq1_d"), Buf("wq2_d")

        O = P.op

        def tap(name, src_ap, reads):
            if name in dbg:
                O("gpsimd", lambda e, s=src_ap, d=dbg[name]: e.dma_start(out=d, in_=s),
                  reads=list(reads), dma=True, out=True)

        O("vector", lambda e: e.memset(eps_sb[:], EPS), writes=[epsb])
        O("vector", lambda e: e.memset(onesall_sb[:], 1.0), writes=[onesb])
        O("sync", lambda e: e.dma_start(out=dmask_sb[:], in_=dmask_d), writes=[dmaskb], dma=True)
        O("sync", lambda e: e.dma_start(out=b1_sb[:], in_=b1_d), writes=[b1b], dma=True)
        O("sync", lambda e: e.dma_start(out=xb_sb[0:1, 0:1024], in_=b2_d), writes=[xb[0]], dma=True)
        for vsl in range(2):
            O("vector", lambda e, vsl=vsl: e.memset(
                vstage_sb[:, vsl * (H * 65):(vsl + 1) * (H * 65)]
                .rearrange("p (h c) -> p h c", h=H)[:, :, 64:65], 1.0),
              writes=[vstage[vsl]])
        for eh in range(2):
            O("tensor", lambda e, eh=eh: e.matmul(
                op_ps[:, 0:512], onesall_sb[0:1, :],
                xb_sb[0:1, eh * 512:(eh + 1) * 512],
                start=True, stop=True), reads=[onesb, xb[0]], writes=[op])
            O("vector", lambda e, eh=eh: e.tensor_copy(
                b2bc_sb[:, eh * 512:(eh + 1) * 512], op_ps[:, 0:512]),
              reads=[op], writes=[b2bc])

        def ln_tile(x_ap, out_ap, x_bufs, out_bufs, sl, act_apply=False):
            st = stats_sb[:, sl * 12:(sl + 1) * 12]
            ag = aggr_sb[:, sl * 2:(sl + 1) * 2]
            rs = rstd_sb[:, sl:sl + 1]
            O("vector", lambda e: e.bn_stats(st[:, 0:6], x_ap[:, 0:512]),
              reads=list(x_bufs), writes=[statsb[sl]])
            O("vector", lambda e: e.bn_stats(st[:, 6:12], x_ap[:, 512:1024]),
              reads=list(x_bufs), writes=[statsb[sl]])
            O("vector", lambda e: e.bn_aggr(ag, st),
              reads=[statsb[sl]], writes=[aggrb[sl]])
            O("scalar", lambda e: e.activation(rs, ag[:, 1:2], AF.Ln, bias=eps_sb[:, 0:1]),
              reads=[aggrb[sl], epsb], writes=[rstdb[sl]])
            O("scalar", lambda e: e.activation(rs, rs, AF.Exp, scale=-0.5),
              reads=[rstdb[sl]], writes=[rstdb[sl]])
            O("vector", lambda e: e.tensor_scalar(out_ap, x_ap, ag[:, 0:1], rs,
                                                  ALU.subtract, ALU.mult),
              reads=list(x_bufs) + [aggrb[sl], rstdb[sl]], writes=list(out_bufs))

        def load_w_steps(wd_ap, dst_sb, dst_buf):
            steps = []
            for step in range(16):
                c, hh = step // 2, step % 2
                sl = step % 2

                def emit(c=c, hh=hh, sl=sl):
                    qv = qscr_sb[:, sl * 512:(sl + 1) * 512]
                    O("sync", lambda e, c=c, hh=hh, qv=qv: e.dma_start(
                        out=qv, in_=wd_ap[c * 128:(c + 1) * 128,
                                          hh * 512:(hh + 1) * 512]),
                      writes=[qscr[sl]], dma=True)
                    O("scalar", lambda e, c=c, hh=hh, qv=qv: e.copy(
                        dst_sb[:, c * 1024 + hh * 512: c * 1024 + (hh + 1) * 512], qv),
                      reads=[qscr[sl]], writes=[dst_buf])
                steps.append(emit)
            return steps

        def load_w_bf16(wd_ap, dst_sb, dst_buf):
            for s_ in load_w_steps(wd_ap, dst_sb, dst_buf):
                s_()

        wq_steps = load_w_steps(wqT_d, wB_sb, wB)
        for g4 in range(4):
            O("sync", lambda e, g4=g4: e.dma_start(
                out=x_sb[:, g4 * 4096:(g4 + 1) * 4096]
                .rearrange("p (t d) -> p t d", t=4),
                in_=xo_d[g4 * 512:(g4 + 1) * 512, :]
                .rearrange("(t p) d -> p t d", p=128)),
              writes=x_t[g4 * 4:(g4 + 1) * 4], dma=True)
            for _ in range(4):
                if wq_steps:
                    wq_steps.pop(0)()
        def ln1b_tile(t):
            if True:
                sl = t % 2
                xa = xb_sb[:, sl * 1024:(sl + 1) * 1024]
                la = lno_sb[:, sl * 1024:(sl + 1) * 1024]
                O("sync", lambda e, t=t, xa=xa: e.dma_start(
                    out=xa, in_=xb_d[t * 128:(t + 1) * 128, :]), writes=[xb[sl]], dma=True)
                ln_tile(xa, la, [xb[sl]], [lno[sl]], sl)
                O("gpsimd", lambda e, t=t, la=la: e.dma_start(
                    out=ln1b_d[t * 128:(t + 1) * 128, :], in_=la),
                  reads=[lno[sl]], writes=[ln1b_g[t // 4]], dma=True)

        def ln1b_group(tt):
            for i in range(4):
                ln1b_tile(4 * tt + i)

        def kv_transposes(tt):
            fs = (tt % 2) * 8
            for dc in range(8):
                O("sync", lambda e, tt=tt, dc=dc, fs=fs: e.dma_start_transpose(
                    fm_sb[:, (fs + dc) * 512:(fs + dc + 1) * 512],
                    ln1b_d[tt * 512:(tt + 1) * 512, dc * 128:(dc + 1) * 128]),
                  reads=[ln1b_g[tt]], writes=[fm[fs + dc]], dma=True)

        load_w_bf16(wqT_d, wB_sb, wB)

        def ln1o_tile(t):
            sl = t % 2
            xa = x_sb[:, t * 1024:(t + 1) * 1024]
            la = lno_sb[:, sl * 1024:(sl + 1) * 1024]
            ln_tile(xa, la, [x_t[t]], [lno[sl]], sl)
            O("gpsimd", lambda e, t=t, la=la: e.dma_start(
                out=ln1o_d[t * 128:(t + 1) * 128, :], in_=la),
              reads=[lno[sl]], writes=[ln1o_g[t // 4]], dma=True)

        def ln1o_group(tt):
            for i in range(4):
                ln1o_tile(4 * tt + i)

        def q_transposes(tt):
            fs = (tt % 2) * 8
            for dc in range(8):
                O("sync", lambda e, tt=tt, dc=dc, fs=fs: e.dma_start_transpose(
                    fm_sb[:, (fs + dc) * 512:(fs + dc + 1) * 512],
                    ln1o_d[tt * 512:(tt + 1) * 512, dc * 128:(dc + 1) * 128]),
                  reads=[ln1o_g[tt]], writes=[fm[fs + dc]], dma=True)

        wk_steps = load_w_steps(wkT_d, wA_sb, wA)
        ln1o_group(0)
        q_transposes(0)
        if TT > 1:
            ln1o_group(1)
            q_transposes(1)
        for tt in range(TT):
            fs = (tt % 2) * 8
            for qc in range(8):
                if qc % 2 == 0:
                    if tt + 2 < TT:
                        ln1o_tile(4 * (tt + 2) + qc // 2)
                    ln1b_tile(4 * tt + qc // 2)
                pslot = qc % 2
                pa = sc_ps[:, pslot * 512:(pslot + 1) * 512]
                for dc in range(8):
                    O("tensor", lambda e, qc=qc, dc=dc, fs=fs, pa=pa: e.matmul(
                        pa, wB_sb[:, dc * 1024 + qc * 128: dc * 1024 + (qc + 1) * 128],
                        fm_sb[:, (fs + dc) * 512:(fs + dc + 1) * 512],
                        start=(dc == 0), stop=(dc == 7)),
                      reads=[wB, fm[fs + dc]], writes=[sc[pslot]])
                O("scalar", lambda e, qc=qc, tt=tt, pa=pa: e.copy(
                    qu_sb[:, qc * 2048 + tt * 512: qc * 2048 + (tt + 1) * 512], pa),
                  reads=[sc[pslot]], writes=[qu])
            if tt + 2 < TT:
                q_transposes(tt + 2)
            for _ in range(4):
                if wk_steps:
                    wk_steps.pop(0)()
        while wk_steps:
            wk_steps.pop(0)()
        q_transposes(0)
        for tt in range(TT):
            fs = (tt % 2) * 8
            if tt + 1 < TT:
                q_transposes(tt + 1)
            for qc in range(8):
                pslot = qc % 2
                pa = sc_ps[:, pslot * 512:(pslot + 1) * 512]
                for dc in range(8):
                    O("tensor", lambda e, qc=qc, dc=dc, fs=fs, pa=pa: e.matmul(
                        pa, wB_sb[:, dc * 1024 + qc * 128: dc * 1024 + (qc + 1) * 128],
                        fm_sb[:, (fs + dc) * 512:(fs + dc + 1) * 512],
                        start=(dc == 0), stop=(dc == 7)),
                      reads=[wB, fm[fs + dc]], writes=[sc[pslot]])
                O("scalar", lambda e, qc=qc, tt=tt, pa=pa: e.copy(
                    qu_sb[:, qc * 2048 + tt * 512: qc * 2048 + (tt + 1) * 512], pa),
                  reads=[sc[pslot]], writes=[qu])

        quant_items = []
        NQ = 2

        def q_abs(c, cc, wd_ap, col, idx):
            sl = idx % NQ
            qv = qscr_sb[:, sl * 512:(sl + 1) * 512]

            def load():
                O("scalar", lambda e, qv=qv: e.dma_start(
                    out=qv, in_=wd_ap[c * 128:(c + 1) * 128, cc * 512:(cc + 1) * 512]),
                  writes=[qscr[sl]], dma=True)

            def compute():
                O("vector", lambda e, qv=qv: e.reduce_sum(
                    asum_sb[:, col:col + 1], qv, axis=AX.X, apply_absolute_value=True),
                  reads=[qscr[sl]], writes=[asum])
            return load, compute

        qidx = 0
        for c in range(8):
            for cc in range(8):
                quant_items.append(q_abs(c, cc, w1T_d, qidx, qidx)); qidx += 1
        for c in range(32):
            for cc in range(2):
                quant_items.append(q_abs(c, cc, w2T_d, qidx, qidx)); qidx += 1

        def q_gamma():
            O("tensor", lambda e: e.matmul(op_ps[0:1, 0:128], onesall_sb[:, 0:1],
                                           asum_sb[:, 0:128], start=True, stop=True),
              reads=[onesb, asum], writes=[op])
            O("vector", lambda e: e.reduce_sum(gam_sb[0:1, 0:1], op_ps[0:1, 0:64],
                                               axis=AX.X),
              reads=[op], writes=[gam])
            O("vector", lambda e: e.reduce_sum(gam_sb[0:1, 1:2], op_ps[0:1, 64:128],
                                               axis=AX.X),
              reads=[op], writes=[gam])
            O("vector", lambda e: e.tensor_single_scalar(
                gam_sb[0:1, 0:2], gam_sb[0:1, 0:2], float(NW) * EPS, ALU.add),
              reads=[gam], writes=[gam])
            O("vector", lambda e: e.reciprocal(gam_sb[0:1, 2:4], gam_sb[0:1, 0:2]),
              reads=[gam], writes=[gam])
            O("vector", lambda e: e.tensor_single_scalar(
                gam_sb[0:1, 2:4], gam_sb[0:1, 2:4], float(NW), ALU.mult),
              reads=[gam], writes=[gam])
            for w in range(2):
                O("tensor", lambda e, w=w: e.matmul(
                    op_ps[:, 128 + w:129 + w], onesall_sb[0:1, :],
                    gam_sb[0:1, 2 + w:3 + w], start=True, stop=True),
                  reads=[onesb, gam], writes=[op])
                O("vector", lambda e, w=w: e.tensor_copy(
                    ginv_sb[:, w:w + 1], op_ps[:, 128 + w:129 + w]),
                  reads=[op], writes=[ginv])

        quant_items.append((lambda: None, q_gamma))

        def q_quant_common(qv, pv, w, store_fn, sl):
            O("vector", lambda e, qv=qv, pv=pv: e.tensor_scalar(
                pv, qv, ginv_sb[:, w:w + 1], 0.5, ALU.mult, ALU.is_gt),
              reads=[qscr[sl], ginv], writes=[qp[sl]])
            nv = qn_sb[:, 0:512]
            O("vector", lambda e, qv=qv, nv=nv: e.tensor_scalar(
                nv, qv, ginv_sb[:, w:w + 1], -0.5, ALU.mult, ALU.is_lt),
              reads=[qscr[sl], ginv], writes=[qn[0]])
            O("vector", lambda e, pv=pv, nv=nv: e.tensor_sub(pv, pv, nv),
              reads=[qp[sl], qn[0]], writes=[qp[sl]])
            store_fn(pv, sl)

        def q_quant_w1(c, cc, idx):
            sl = idx % NQ
            qv = qscr_sb[:, sl * 512:(sl + 1) * 512]
            pv = qp_sb[:, sl * 512:(sl + 1) * 512]

            def load():
                O("scalar", lambda e, qv=qv: e.dma_start(
                    out=qv, in_=w1T_d[c * 128:(c + 1) * 128, cc * 512:(cc + 1) * 512]),
                  writes=[qscr[sl]], dma=True)

            def store(pv, sl):
                O("gpsimd", lambda e, pv=pv, c=c, cc=cc: e.dma_start(
                    out=wq1_d[cc * 4:(cc + 1) * 4, :, c, :].rearrange(
                        "fc p f -> p fc f"),
                    in_=pv.rearrange("p (fc f) -> p fc f", fc=4)),
                  reads=[qp[sl]], writes=[wq1], dma=True)

            return load, (lambda: q_quant_common(qv, pv, 0, store, sl))

        def q_quant_w2(c, cc, idx):
            sl = idx % NQ
            qv = qscr_sb[:, sl * 512:(sl + 1) * 512]
            pv = qp_sb[:, sl * 512:(sl + 1) * 512]

            def load():
                O("scalar", lambda e, qv=qv: e.dma_start(
                    out=qv, in_=w2T_d[c * 128:(c + 1) * 128, cc * 512:(cc + 1) * 512]),
                  writes=[qscr[sl]], dma=True)

            def store(pv, sl):
                O("gpsimd", lambda e, pv=pv, c=c, cc=cc: e.dma_start(
                    out=wq2_d[c * 128:(c + 1) * 128, cc * 512:(cc + 1) * 512], in_=pv),
                  reads=[qp[sl]], writes=[wq2], dma=True)

            return load, (lambda: q_quant_common(qv, pv, 1, store, sl))

        for c in range(8):
            for cc in range(8):
                quant_items.append(q_quant_w1(c, cc, qidx)); qidx += 1
        for c in range(32):
            for cc in range(2):
                quant_items.append(q_quant_w2(c, cc, qidx)); qidx += 1

        q_load_next = [0]
        q_comp_next = [0]
        QLEAD = NQ

        def quant_step(n):
            for _ in range(n):
                if (q_load_next[0] - q_comp_next[0] >= QLEAD
                        or (q_load_next[0] >= len(quant_items)
                            and q_comp_next[0] < len(quant_items))):
                    quant_items[q_comp_next[0]][1]()
                    q_comp_next[0] += 1
                if q_load_next[0] < len(quant_items):
                    quant_items[q_load_next[0]][0]()
                    q_load_next[0] += 1

        wv_steps = load_w_steps(wvT_d, wB_sb, wB)
        vs_ct = 0

        kv_transposes(0)
        kv_transposes(1)
        for tt in range(S // 512):
            fs = (tt % 2) * 8
            for kc in range(8):
                if wv_steps and kc % 2 == 1:
                    for _ in range(4):
                        if wv_steps:
                            wv_steps.pop(0)()
                if kc % 2 == 0 and tt + 4 < S // 512:
                    ln1b_tile(4 * (tt + 4) + kc // 2)
                pslot = kc % 4
                pa = sc_ps[:, pslot * 512:(pslot + 1) * 512]
                ke = kevac_sb[:, (kc % 2) * 512:(kc % 2 + 1) * 512]
                for dc in range(8):
                    O("tensor", lambda e, kc=kc, dc=dc, fs=fs, pa=pa: e.matmul(
                        pa, wA_sb[:, dc * 1024 + kc * 128: dc * 1024 + (kc + 1) * 128],
                        fm_sb[:, (fs + dc) * 512:(fs + dc + 1) * 512],
                        start=(dc == 0), stop=(dc == 7)),
                      reads=[wA, fm[fs + dc]], writes=[sc[pslot]])
                O("scalar", lambda e, ke=ke, pa=pa: e.copy(ke, pa),
                  reads=[sc[pslot]], writes=[kevac[kc % 2]])
                O("gpsimd", lambda e, kc=kc, tt=tt, ke=ke: e.dma_start(
                    out=kfm_d[kc * 128:(kc + 1) * 128, tt * 512:(tt + 1) * 512],
                    in_=ke),
                  reads=[kevac[kc % 2]], writes=[kfm], dma=True)
            for ts in range(4):
                vsl = vs_ct % 2
                vs_ct += 1
                va = vstage_sb[:, vsl * (H * 65):(vsl + 1) * (H * 65)]
                for vh in range(2):
                    pa = oa_ps[:, vh * 512:(vh + 1) * 512]
                    for dc in range(8):
                        O("tensor", lambda e, ts=ts, vh=vh, dc=dc, fs=fs, pa=pa: e.matmul(
                            pa, fm_sb[:, (fs + dc) * 512 + ts * 128:
                                      (fs + dc) * 512 + (ts + 1) * 128],
                            wB_sb[:, dc * 1024 + vh * 512: dc * 1024 + (vh + 1) * 512],
                            start=(dc == 0), stop=(dc == 7)),
                          reads=[wB, fm[fs + dc]], writes=[oa[vh]])
                    O("scalar", lambda e, vh=vh, va=va, pa=pa: e.copy(
                        va.rearrange("p (h c) -> p h c", h=H)[:, vh * 8:(vh + 1) * 8, 0:64],
                        pa.rearrange("p (h c) -> p h c", h=8)),
                      reads=[oa[vh]], writes=[vstage[vsl]])
                O("gpsimd", lambda e, tt=tt, ts=ts, va=va: e.dma_start(
                    out=vtm_d[tt * 512 + ts * 128: tt * 512 + (ts + 1) * 128, :],
                    in_=va[:, 0:H * 65] if False else va),
                  reads=[vstage[vsl]], writes=[vtm], dma=True)
            if tt + 2 < S // 512:
                kv_transposes(tt + 2)
        tap("ln1b", ln1b_d, ln1b_g)
        tap("kfm", kfm_d, [kfm])
        tap("vtm", vtm_d, [vtm])

        wo_steps = load_w_steps(woT_d, wA_sb, wA)

        ln3_q = []
        O("vector", lambda e: e.memset(op_ps[:, 0:1024], 0.0), writes=[op])
        gb_ct = 0

        def band_load(T, hp, b):
            nonlocal gb_ct
            sl = gb_ct % 2
            gb_ct += 1
            O("sync", lambda e, hp=hp, b=b, sl=sl: e.dma_start(
                out=kband_sb[:, sl * 1024:(sl + 1) * 1024],
                in_=kfm_d[hp * 128:(hp + 1) * 128, b * 1024:(b + 1) * 1024]),
              reads=[kfm], writes=[kband[sl]], dma=True)
            O("sync", lambda e, hp=hp, b=b, sl=sl: e.dma_start(
                out=vband_sb[:, sl * 1040:(sl + 1) * 1040]
                .rearrange("p (kt c) -> p kt c", kt=8),
                in_=vtm_d[b * 1024:(b + 1) * 1024, hp * 130:(hp + 1) * 130]
                .rearrange("(kt p) c -> p kt c", p=128)),
              reads=[vtm], writes=[vband[sl]], dma=True)
            return sl

        for T in range(TT):
            kvr = 2048 * (T + 1)
            nb = kvr // 1024
            nkb = kvr // 128
            blocks = [(hp, b) for hp in range(8) for b in range(nb)]
            slot_of = {}
            slot_of[blocks[0]] = band_load(T, *blocks[0])
            if len(blocks) > 1:
                slot_of[blocks[1]] = band_load(T, *blocks[1])
            pairs = [
                (sc_ps[:, 0:1024], [sc[0], sc[1]], expT_sb[:, 0:1024],
                 [expT[0], expT[1]]),
                (sc_ps[:, 1024:2048], [sc[2], sc[3]], expT_sb[:, 1024:2048],
                 [expT[2], expT[3]]),
                (op_ps[:, 0:1024], [op], kevac_sb[:, 0:1024],
                 [kevac[0], kevac[1]]),
            ]
            av_q = []
            uc = [0]
            epi_q = []

            def flush_av(n):
                while len(av_q) > n:
                    av_q.pop(0)()

            for bi, (hp, b) in enumerate(blocks):
                sl = slot_of.pop((hp, b))
                for g8 in range(8):
                    kb = 8 * b + g8
                    j = kb - 16 * T
                    qoff = 32 * j if j >= 0 else 0
                    pp, pbufs, ee, ebufs = pairs[uc[0] % 3]
                    uc[0] += 1
                    for h2 in range(2):
                        O("tensor", lambda e, h2=h2, hp=hp, kb=kb, T=T,
                          sl=sl, qoff=qoff, pp=pp: e.matmul(
                            pp[:, h2 * 512 + qoff:(h2 + 1) * 512],
                            kband_sb[64 * h2: 64 * h2 + 64,
                                     sl * 1024 + (kb % 8) * 128:
                                     sl * 1024 + (kb % 8 + 1) * 128],
                            qu_sb[64 * h2: 64 * h2 + 64,
                                  hp * 2048 + T * 512 + qoff:
                                  hp * 2048 + (T + 1) * 512],
                            start=True, stop=True),
                          reads=[kband[sl], qu], writes=pbufs)
                    flush_av(2)
                    if j >= 3:
                        for h2 in range(2):
                            O("scalar", lambda e, h2=h2, qoff=qoff, pp=pp,
                              ee=ee: e.activation(
                                ee[:, h2 * 512 + qoff:(h2 + 1) * 512],
                                pp[:, h2 * 512 + qoff:(h2 + 1) * 512],
                                AF.Exp, scale=0.125),
                              reads=pbufs, writes=ebufs)
                    else:
                        O("scalar", lambda e, pp=pp, ee=ee: e.activation(
                            ee, pp, AF.Exp, scale=0.125),
                          reads=pbufs, writes=ebufs)
                    if j >= 0:
                        for h2 in range(2):
                            O("vector", lambda e, h2=h2, qoff=qoff, ee=ee: e.tensor_mul(
                                ee[:, h2 * 512 + qoff: h2 * 512 + qoff + 32],
                                ee[:, h2 * 512 + qoff: h2 * 512 + qoff + 32],
                                dmask_sb[:]),
                              reads=ebufs + [dmaskb], writes=ebufs)
                    def make_av(kb=kb, sl=sl, qoff=qoff, nkb=nkb, ee=ee,
                                ebufs=ebufs):
                        def emit_av():
                            for h2 in range(2):
                                O("tensor", lambda e, h2=h2, kb=kb, sl=sl,
                                  qoff=qoff, st=(kb == 0), ee=ee,
                                  sp=(kb == nkb - 1): e.matmul(
                                    oa_ps[0:65, h2 * 512 + qoff:(h2 + 1) * 512],
                                    vband_sb[:, sl * 1040 + (kb % 8) * 130 + h2 * 65:
                                             sl * 1040 + (kb % 8) * 130 + (h2 + 1) * 65],
                                    ee[:, h2 * 512 + qoff:(h2 + 1) * 512],
                                    start=st, stop=sp),
                                  reads=[vband[sl]] + ebufs, writes=[oa[h2]])
                        return emit_av

                    av_q.append(make_av())
                if bi + 2 < len(blocks):
                    flush_av(0)
                    slot_of[blocks[bi + 2]] = band_load(T, *blocks[bi + 2])
                if epi_q:
                    epi_q.pop(0)()
                if wo_steps:
                    wo_steps.pop(0)()
                    if wo_steps:
                        wo_steps.pop(0)()
                else:
                    quant_step(2)
                if ln3_q:
                    ln3_q.pop(0)()

                if b == nb - 1:
                    flush_av(0)
                    O("vector", lambda e: e.tensor_copy(
                        zscr_sb[0:65, 0:1024], oa_ps[0:65, 0:1024]),
                      reads=[oa[0], oa[1]], writes=[zscr, onf])

                    def make_epi(hp=hp):
                        def emit_epi():
                            O("scalar", lambda e: e.activation(
                                zscr_sb[64:65, 0:1024], zscr_sb[64:65, 0:1024],
                                AF.Ln),
                              reads=[zscr], writes=[zscr])
                            O("scalar", lambda e: e.activation(
                                zscr_sb[64:65, 0:1024], zscr_sb[64:65, 0:1024],
                                AF.Exp, scale=-1.0),
                              reads=[zscr], writes=[zscr])
                            for h2 in range(2):
                                hs = h2 * 512
                                O("tensor", lambda e, hs=hs: e.matmul(
                                    op_ps[0:64, hs:hs + 512],
                                    onesall_sb[64:65, 0:64],
                                    zscr_sb[64:65, hs:hs + 512],
                                    start=True, stop=True),
                                  reads=[onesb, zscr], writes=[op])
                            O("vector", lambda e, hp=hp: e.tensor_mul(
                                onorm_sb[0:64, hp * 512:(hp + 1) * 512],
                                onf_sb[0:64, 0:512], op_ps[0:64, 0:512]),
                              reads=[onf, op], writes=[onorm[hp]])
                            O("vector", lambda e: e.tensor_mul(
                                ontmp_sb[0:64, 0:512], onf_sb[0:64, 512:1024],
                                op_ps[0:64, 512:1024]),
                              reads=[onf, op], writes=[w1t[0]])
                            O("gpsimd", lambda e, hp=hp: e.dma_start(
                                out=onorm_sb[64:128, hp * 512:(hp + 1) * 512],
                                in_=ontmp_sb[0:64, 0:512]),
                              reads=[w1t[0]], writes=[onorm[hp]], dma=True)
                        return emit_epi

                    epi_q.append(make_epi())
            while epi_q:
                epi_q.pop(0)()
            for qs in range(4):
                xt = 4 * T + qs
                xa = x_sb[:, xt * 1024:(xt + 1) * 1024]
                if qs % 2 == 0:
                    pfull, pwb = op_ps[:, 0:1024], [op]
                else:
                    pfull, pwb = sc_ps[:, 0:1024], [sc[0], sc[1]]
                for pr in range(8):
                    for eh in range(2):
                        O("tensor", lambda e, qs=qs, eh=eh, pr=pr, pfull=pfull: e.matmul(
                            pfull[:, eh * 512:(eh + 1) * 512],
                            onorm_sb[:, pr * 512 + qs * 128: pr * 512 + (qs + 1) * 128],
                            wA_sb[:, pr * 1024 + eh * 512: pr * 1024 + (eh + 1) * 512],
                            start=(pr == 0), stop=(pr == 7)),
                          reads=[onorm[pr], wA], writes=pwb)
                sl = qs % 2
                st = stats_sb[:, sl * 12:(sl + 1) * 12]
                ag = aggr_sb[:, sl * 2:(sl + 1) * 2]
                rs = rstd_sb[:, sl:sl + 1]
                O("vector", lambda e, pfull=pfull, st=st: e.bn_stats(
                    st[:, 0:6], pfull[:, 0:512]), reads=pwb, writes=[statsb[sl]])
                O("vector", lambda e, pfull=pfull, st=st: e.bn_stats(
                    st[:, 6:12], pfull[:, 512:1024]), reads=pwb, writes=[statsb[sl]])
                O("vector", lambda e, st=st, ag=ag: e.bn_aggr(ag, st),
                  reads=[statsb[sl]], writes=[aggrb[sl]])
                O("scalar", lambda e, ag=ag, rs=rs: e.activation(
                    rs, ag[:, 1:2], AF.Ln, bias=eps_sb[:, 0:1]),
                  reads=[aggrb[sl], epsb], writes=[rstdb[sl]])
                O("scalar", lambda e, rs=rs: e.activation(rs, rs, AF.Exp, scale=-0.5),
                  reads=[rstdb[sl]], writes=[rstdb[sl]])
                O("vector", lambda e, pfull=pfull, ag=ag, rs=rs: e.tensor_scalar(
                    lntmp_sb[:, 0:1024], pfull, ag[:, 0:1], rs, ALU.subtract, ALU.mult),
                  reads=pwb + [aggrb[sl], rstdb[sl]], writes=[lntmpb])
                O("gpsimd", lambda e, xa=xa: e.tensor_add(xa, xa, lntmp_sb[:, 0:1024]),
                  reads=[x_t[xt], lntmpb], writes=[x_t[xt]])
                def make_ln3(xt=xt, sl=sl, T=T):
                    def emit_ln3():
                        xa2 = x_sb[:, xt * 1024:(xt + 1) * 1024]
                        la = lno_sb[:, sl * 1024:(sl + 1) * 1024]
                        ln_tile(xa2, la, [x_t[xt]], [lno[sl]], sl)
                        O("gpsimd", lambda e, xt=xt, la=la: e.dma_start(
                            out=ln3_d[xt * 128:(xt + 1) * 128, :], in_=la),
                          reads=[lno[sl]], writes=[ln3_g[T]], dma=True)
                    return emit_ln3

                ln3_q.append(make_ln3())
        while ln3_q:
            ln3_q.pop(0)()
        tap("ln3", ln3_d, ln3_g)
        while q_comp_next[0] < len(quant_items):
            quant_step(4)
        tap("wq1", wq1_d, [wq1])
        tap("wq2", wq2_d, [wq2])

        def ffn_transposes(tt):
            fs = (tt % 2) * 8
            for dc in range(8):
                O("sync", lambda e, tt=tt, dc=dc, fs=fs: e.dma_start_transpose(
                    fm_sb[:, (fs + dc) * 512:(fs + dc + 1) * 512],
                    ln3_d[tt * 512:(tt + 1) * 512, dc * 128:(dc + 1) * 128]),
                  reads=[ln3_g[tt]], writes=[fm[fs + dc]], dma=True)

        ffn_transposes(0)
        ffn_q = []
        w1s = [(w1t_sb[:, 0:1024], w1t[0]), (w1t_sb[:, 1024:2048], w1t[1]),
               (kband_sb[:, 0:1024], kband[0]), (kband_sb[:, 1024:2048], kband[1])]
        w2s = [(w2t_sb[:, 0:1024], w2t[0]), (w2t_sb[:, 1024:2048], w2t[1]),
               (vband_sb[:, 0:1024], vstage[0]), (vband_sb[:, 1040:2064], vstage[1])]
        for tt in range(TT):
            fs = (tt % 2) * 8
            if tt + 1 < TT:
                ffn_transposes(tt + 1)
            for fc in range(32):
                if fc == 0:
                    while len(ffn_q) > 2:
                        ffn_q.pop(0)()
                elif fc in (8, 16) and ffn_q:
                    ffn_q.pop(0)()
                wv, wb = w1s[fc % 4]
                O("sync", lambda e, fc=fc, wv=wv: e.dma_start(
                    out=wv, in_=wq1_d[fc, :, :, :].rearrange("p c f -> p (c f)")),
                  reads=[wq1], writes=[wb], dma=True)
                ua = oa_ps[:, (fc % 2) * 512:(fc % 2 + 1) * 512]
                for dc in range(8):
                    O("tensor", lambda e, fc=fc, dc=dc, fs=fs, wv=wv, ua=ua: e.matmul(
                        ua, wv[:, dc * 128:(dc + 1) * 128],
                        fm_sb[:, (fs + dc) * 512:(fs + dc + 1) * 512],
                        start=(dc == 0), stop=(dc == 7)),
                      reads=[wb, fm[fs + dc]], writes=[oa[fc % 2]])
                O("vector", lambda e, fc=fc, ua=ua: e.tensor_scalar(
                    qu_sb[:, fc * 512:(fc + 1) * 512], ua,
                    b1_sb[:, fc:fc + 1], 0.0, ALU.add, ALU.max),
                  reads=[oa[fc % 2], b1b], writes=[qu])
            pmap = {(0, 0): (sc_ps[:, 0:512], sc[0]), (0, 1): (sc_ps[:, 512:1024], sc[1]),
                    (0, 2): (sc_ps[:, 1024:1536], sc[2]), (0, 3): (sc_ps[:, 1536:2048], sc[3]),
                    (1, 0): (oa_ps[:, 0:512], oa[0]), (1, 1): (oa_ps[:, 512:1024], oa[1]),
                    (1, 2): (op_ps[:, 0:512], op), (1, 3): (op_ps[:, 512:1024], op)}
            for fc in range(32):
                wv, wb = w2s[fc % 4]
                O("sync", lambda e, fc=fc, wv=wv: e.dma_start(
                    out=wv, in_=wq2_d[fc * 128:(fc + 1) * 128, :]),
                  reads=[wq2], writes=[wb], dma=True)
                for ts in range(4):
                    for eh in range(2):
                        pa, pb = pmap[(eh, ts)]
                        O("tensor", lambda e, fc=fc, eh=eh, ts=ts, pa=pa, wv=wv: e.matmul(
                            pa, qu_sb[:, fc * 512 + ts * 128: fc * 512 + (ts + 1) * 128],
                            wv[:, eh * 512:(eh + 1) * 512],
                            start=(fc == 0), stop=(fc == 31)),
                          reads=[qu, wb], writes=[pb])
            for ts in range(4):
                def make_ln4(tt=tt, ts=ts):
                    def emit_ln4():
                        xt = 4 * tt + ts
                        xa = x_sb[:, xt * 1024:(xt + 1) * 1024]
                        ff = ffno_sb[:, 1024:2048]
                        for eh in (1, 0):
                            pa, pb = pmap[(eh, ts)]
                            O("vector", lambda e, pa=pa, eh=eh, ff=ff: e.tensor_add(
                                ff[:, eh * 512:(eh + 1) * 512], pa,
                                b2bc_sb[:, eh * 512:(eh + 1) * 512]),
                              reads=[pb, b2bc], writes=[xb[1]])
                        sl = ts % 2
                        st = stats_sb[:, sl * 12:(sl + 1) * 12]
                        ag = aggr_sb[:, sl * 2:(sl + 1) * 2]
                        rs = rstd_sb[:, sl:sl + 1]
                        O("vector", lambda e, ff=ff, st=st: e.bn_stats(
                            st[:, 0:6], ff[:, 0:512]),
                          reads=[xb[1]], writes=[statsb[sl]])
                        O("vector", lambda e, ff=ff, st=st: e.bn_stats(
                            st[:, 6:12], ff[:, 512:1024]),
                          reads=[xb[1]], writes=[statsb[sl]])
                        O("vector", lambda e, st=st, ag=ag: e.bn_aggr(ag, st),
                          reads=[statsb[sl]], writes=[aggrb[sl]])
                        O("scalar", lambda e, ag=ag, rs=rs: e.activation(
                            rs, ag[:, 1:2], AF.Ln, bias=eps_sb[:, 0:1]),
                          reads=[aggrb[sl], epsb], writes=[rstdb[sl]])
                        O("scalar", lambda e, rs=rs: e.activation(
                            rs, rs, AF.Exp, scale=-0.5),
                          reads=[rstdb[sl]], writes=[rstdb[sl]])
                        O("vector", lambda e, ff=ff, ag=ag, rs=rs: e.tensor_scalar(
                            lntmp_sb[:, 0:1024], ff, ag[:, 0:1], rs, ALU.subtract, ALU.mult),
                          reads=[xb[1], aggrb[sl], rstdb[sl]], writes=[lntmpb])
                        O("gpsimd", lambda e, xa=xa: e.tensor_add(xa, xa, lntmp_sb[:, 0:1024]),
                          reads=[x_t[xt], lntmpb], writes=[x_t[xt]])
                    return emit_ln4

                ffn_q.append(make_ln4())
        while ffn_q:
            ffn_q.pop(0)()

        O("sync", lambda e: e.dma_start(
            out=out_d.rearrange("(t p) d -> p t d", p=128),
            in_=x_sb[:].rearrange("p (t d) -> p t d", t=NT)),
          reads=x_t, dma=True, out=True)

        with contextlib.ExitStack() as es2:
            sems = {e: es2.enter_context(nc.semaphore("s_" + e)) for e in Prog.ENGS}
            dsems = {}
            for q in Prog.DMAQ:
                for s in range(Prog.R):
                    dsems[(q, s)] = es2.enter_context(nc.semaphore(f"d_{q}{s}"))
            with nc.Block() as block:
                P.emit(block, sems, dsems)

    return nc


def _stage_inputs(inputs):
    x = np.asarray(inputs["L0_attn_x"], dtype=np.float32)
    wq = np.asarray(inputs["L0_attn_q_heads"], dtype=np.float32)
    wk = np.asarray(inputs["L0_attn_k_heads"], dtype=np.float32)
    wv = np.asarray(inputs["L0_attn_v_heads"], dtype=np.float32)
    wo = np.asarray(inputs["L0_attn_output_proj"], dtype=np.float32)
    w1 = np.asarray(inputs["L0_ffn_W1"], dtype=np.float32)
    b1 = np.asarray(inputs["L0_ffn_b1"], dtype=np.float32)
    w2 = np.asarray(inputs["L0_ffn_W2"], dtype=np.float32)
    b2 = np.asarray(inputs["L0_ffn_b2"], dtype=np.float32)

    wqT = np.ascontiguousarray(wq.T)
    wkT = np.ascontiguousarray(wk.T)
    wvT = np.ascontiguousarray(wv.T)
    woT = np.ascontiguousarray(wo.T)
    w1T = np.ascontiguousarray(w1.T)
    w2T = np.ascontiguousarray(w2.T)
    b2r = np.ascontiguousarray(b2.reshape(1, D))
    b1r = np.ascontiguousarray(b1.reshape(FF // 128, 128).T)

    in_maps = []
    for c in range(8):
        b = c // 4
        r = c % 4
        kk, ttau = np.meshgrid(np.arange(128), np.arange(32), indexing="ij")
        dmask = (kk <= 4 * ttau + r).astype(ml_dtypes.bfloat16)
        xb = np.ascontiguousarray(x[b])
        xo = np.ascontiguousarray(x[b, r::4])
        in_maps.append({
            "x_batch": xb, "x_own": xo,
            "wqT": wqT, "wkT": wkT, "wvT": wvT, "woT": woT,
            "w1T": w1T, "w2T": w2T, "b1": b1r, "b2": b2r,
            "dmask": np.ascontiguousarray(dmask),
        })
    return in_maps


def kernel(**inputs):
    in_maps = _stage_inputs(inputs)
    nc = build_program()
    res = run_bass_kernel_spmd(nc, in_maps, core_ids=list(range(8)))
    out = np.empty((B, S, D), dtype=np.float32)
    for c in range(8):
        b, r = c // 4, c % 4
        out[b, r::4, :] = res.results[c]["out"]
    return out


# version 30
# speedup vs baseline: 1.1230x; 1.0156x over previous
import contextlib
import numpy as np
import ml_dtypes
import concourse.bass as bass
import concourse.mybir as mybir
from concourse.bass_utils import run_bass_kernel_spmd

F32 = mybir.dt.float32
BF16 = mybir.dt.bfloat16
AF = mybir.ActivationFunctionType
ALU = mybir.AluOpType
AX = mybir.AxisListType

B, S, D = 2, 8192, 1024
H, DK = 16, 64
FF = 4096
TOK = S // 4
NT = TOK // 128
TT = TOK // 512
EPS = 1e-5
NW = FF * D

_DEBUG_TAPS = {}


class Buf:
    __slots__ = ("w", "r", "name")

    def __init__(self, name=""):
        self.w = None
        self.r = []
        self.name = name


class Prog:
    ENGS = ("tensor", "vector", "scalar", "gpsimd", "sync")
    DMAQ = ("sync", "gpsimd", "scalar")
    R = 8

    def __init__(self):
        self.ops = []
        self.eng_ops = {e: [] for e in self.ENGS}
        self.dma_use = {}

    def op(self, eng, fn, reads=(), writes=(), dma=False, out=False):
        i = len(self.ops)
        deps = set()
        for b in reads:
            if b.w is not None:
                deps.add(b.w)
            b.r.append(i)
        for b in writes:
            if b.w is not None:
                deps.add(b.w)
            deps.update(b.r)
            b.w = i
            b.r = []
        deps.discard(i)
        o = dict(eng=eng, fn=fn, deps=deps, dma=dma, out=out, slot=None, use=None)
        if dma:
            assert eng in self.DMAQ, f"dma on engine {eng}"
            nq = sum(self.dma_use.get((eng, s), 0) for s in range(self.R))
            slot = nq % self.R
            use = self.dma_use.get((eng, slot), 0)
            self.dma_use[(eng, slot)] = use + 1
            o["slot"], o["use"] = slot, use
        self.ops.append(o)
        self.eng_ops[eng].append(i)
        return i

    def emit(self, block, sems, dsems):
        n = len(self.ops)
        signal = [False] * n
        for o in self.ops:
            for d in o["deps"]:
                od = self.ops[d]
                if od["eng"] != o["eng"] or o["dma"] or od["dma"] or od["eng"] != "tensor":
                    signal[d] = True
        for i, o in enumerate(self.ops):
            if o["out"] or o["dma"]:
                signal[i] = True
        semval = [0] * n
        cnt = {e: 0 for e in self.ENGS}
        for i, o in enumerate(self.ops):
            if o["dma"]:
                semval[i] = 16 * (o["use"] + 1)
            else:
                if signal[i]:
                    cnt[o["eng"]] += 1
                semval[i] = cnt[o["eng"]]
        self.final_cnt = cnt

        def wait_key(d):
            od = self.ops[d]
            if od["dma"]:
                return (od["eng"], od["slot"]), semval[d]
            return od["eng"], semval[d]

        def sem_of(key):
            return dsems[key] if isinstance(key, tuple) else sems[key]

        for eng in self.ENGS:
            idxs = self.eng_ops[eng]
            if not idxs:
                continue

            def body(engine, idxs=idxs, eng=eng):
                last_wait = {}
                for i in idxs:
                    o = self.ops[i]
                    waits = {}
                    for d in o["deps"]:
                        od = self.ops[d]
                        if (od["eng"] == eng == "tensor"
                                and not od["dma"] and not o["dma"]):
                            continue
                        k, v = wait_key(d)
                        if v > waits.get(k, 0):
                            waits[k] = v
                    if o["dma"] and o["use"] >= 1:
                        k = (eng, o["slot"])
                        v = 16 * o["use"]
                        if v > waits.get(k, 0):
                            waits[k] = v
                    for k in sorted(waits, key=str):
                        v = waits[k]
                        if last_wait.get(k, 0) >= v:
                            continue
                        last_wait[k] = v
                        engine.wait_ge(sem_of(k), v)
                    ins = o["fn"](engine)
                    if signal[i]:
                        if o["dma"]:
                            ins.then_inc(dsems[(eng, o["slot"])], 16)
                        else:
                            ins.then_inc(sems[eng], 1)

            getattr(block, eng)(body)

        @block.gpsimd
        def _(gpsimd):
            for e in self.ENGS:
                if self.final_cnt[e]:
                    gpsimd.wait_ge(sems[e], self.final_cnt[e])
            for (q, s), u in sorted(self.dma_use.items()):
                gpsimd.wait_ge(dsems[(q, s)], 16 * u)


def build_program():
    nc = bass.Bass("TRN2", target_bir_lowering=False)
    P = Prog()

    xb_d = nc.dram_tensor("x_batch", [S, D], F32, kind="ExternalInput").ap()
    xo_d = nc.dram_tensor("x_own", [TOK, D], F32, kind="ExternalInput").ap()
    wqT_d = nc.dram_tensor("wqT", [D, D], F32, kind="ExternalInput").ap()
    wkT_d = nc.dram_tensor("wkT", [D, D], F32, kind="ExternalInput").ap()
    wvT_d = nc.dram_tensor("wvT", [D, D], F32, kind="ExternalInput").ap()
    woT_d = nc.dram_tensor("woT", [D, D], F32, kind="ExternalInput").ap()
    w1T_d = nc.dram_tensor("w1T", [D, FF], F32, kind="ExternalInput").ap()
    w2T_d = nc.dram_tensor("w2T", [FF, D], F32, kind="ExternalInput").ap()
    b1_d = nc.dram_tensor("b1", [128, FF // 128], F32, kind="ExternalInput").ap()
    b2_d = nc.dram_tensor("b2", [1, D], F32, kind="ExternalInput").ap()
    dmask_d = nc.dram_tensor("dmask", [128, 32], BF16, kind="ExternalInput").ap()

    out_d = nc.dram_tensor("out", [TOK, D], F32, kind="ExternalOutput").ap()

    ln1b_d = nc.dram_tensor("ln1b", [S, D], BF16).ap()
    ln1o_d = nc.dram_tensor("ln1o", [TOK, D], BF16).ap()
    ln3_d = nc.dram_tensor("ln3", [TOK, D], BF16).ap()
    kfm_d = nc.dram_tensor("kfm", [D, S], BF16).ap()
    vtm_d = nc.dram_tensor("vtm", [S, H * 65], BF16).ap()
    wq1_d = nc.dram_tensor("wq1", [FF // 128, 128, 8, 128], BF16).ap()
    wq2_d = nc.dram_tensor("wq2", [FF, D], BF16).ap()

    dbg = {}
    for name, (shape, dt) in _DEBUG_TAPS.items():
        dbg[name] = nc.dram_tensor("dbg_" + name, shape, dt, kind="ExternalOutput").ap()

    es = contextlib.ExitStack()

    def sb(name, shape, dt=F32):
        return es.enter_context(nc.sbuf_tensor(name, shape, dt))

    with es:
        x_sb = sb("x_sb", [128, NT * 1024])
        qu_sb = sb("qu_sb", [128, 16384], BF16)
        fm_sb = sb("fm_sb", [128, 16 * 512], BF16)
        wA_sb = sb("wA_sb", [128, 8 * 1024], BF16)
        wB_sb = sb("wB_sb", [128, 8 * 1024], BF16)
        kband_sb = sb("kband_sb", [128, 2 * 1024], BF16)
        vband_sb = sb("vband_sb", [128, 2 * 1040], BF16)
        expT_sb = sb("expT_sb", [128, 2048], BF16)
        onorm_sb = sb("onorm_sb", [128, 8 * 512], BF16)
        ontmp_sb = sb("ontmp_sb", [64, 512], BF16)
        xb_sb = sb("xb_sb", [128, 2048])
        lno_sb = sb("lno_sb", [128, 2048], BF16)
        kevac_sb = sb("kevac_sb", [128, 1024], BF16)
        vstage_sb = sb("vstage_sb", [128, 2 * (H * 65)], BF16)
        qn_sb = vstage_sb
        w1t_sb = sb("w1t_sb", [128, 2 * 1024], BF16)
        ontmp_sb = w1t_sb
        w2t_sb = sb("w2t_sb", [128, 2 * 1024], BF16)
        b1_sb = sb("b1_sb", [128, FF // 128])
        b2bc_sb = sb("b2bc_sb", [128, D], BF16)
        dmask_sb = sb("dmask_sb", [128, 32], BF16)
        stats_sb = sb("stats_sb", [128, 24])
        aggr_sb = sb("aggr_sb", [128, 4])
        rstd_sb = sb("rstd_sb", [128, 2])
        eps_sb = sb("eps_sb", [128, 1])
        onesall_sb = sb("onesall_sb", [128, 128])
        zscr_sb = sb("zscr_sb", [128, 1024])
        onf_sb = zscr_sb
        qscr_sb = sb("qscr_sb", [128, 2 * 512])
        qp_sb = sb("qp_sb", [128, 2 * 512], BF16)
        asum_sb = sb("asum_sb", [128, 128])
        gam_sb = sb("gam_sb", [1, 64])
        ginv_sb = sb("ginv_sb", [128, 2])
        ffno_sb = xb_sb
        lntmp_sb = xb_sb

        sc_ps = es.enter_context(nc.psum_tensor("sc_ps", [128, 2048], F32))
        oa_ps = es.enter_context(nc.psum_tensor("oa_ps", [128, 1024], F32))
        op_ps = es.enter_context(nc.psum_tensor("op_ps", [128, 1024], F32))

        x_t = [Buf(f"x{t}") for t in range(NT)]
        qu = Buf("qu")
        fm = [Buf(f"fm{i}") for i in range(16)]
        wA, wB = Buf("wA"), Buf("wB")
        kband = [Buf("kb0"), Buf("kb1")]
        vband = [Buf("vb0"), Buf("vb1")]
        expT = [Buf(f"exp{i}") for i in range(4)]
        onorm = [Buf(f"on{pr}") for pr in range(8)]
        ontmp = None
        xb = [Buf("xb0"), Buf("xb1")]
        lno = [Buf("lno0"), Buf("lno1")]
        kevac = [Buf("ke0"), Buf("ke1")]
        vstage = [Buf("vs0"), Buf("vs1")]
        w1t = [Buf("w1t0"), Buf("w1t1")]
        w2t = [Buf("w2t0"), Buf("w2t1")]
        b1b, b2bc = Buf("b1"), Buf("b2bc")
        dmaskb = Buf("dmask")
        statsb = [Buf("st0"), Buf("st1")]
        aggrb = [Buf("ag0"), Buf("ag1")]
        rstdb = [Buf("rs0"), Buf("rs1")]
        epsb, onesb = Buf("eps"), Buf("ones")
        onf, zscr = Buf("onf"), Buf("zscr")
        qscr = [Buf(f"qs{i}") for i in range(2)]
        qp = [Buf(f"qp{i}") for i in range(2)]
        qn = vstage
        asum, gam, ginv = Buf("asum"), Buf("gam"), Buf("ginv")
        lntmpb = xb[0]
        sc = [Buf(f"sc{i}") for i in range(4)]
        oa = [Buf("oa0"), Buf("oa1")]
        op = Buf("op")
        ln1b_g = [Buf(f"l1b{g}") for g in range(16)]
        ln1o_g = [Buf(f"l1o{g}") for g in range(4)]
        ln3_g = [Buf(f"l3{g}") for g in range(4)]
        kfm, vtm = Buf("kfm_d"), Buf("vtm_d")
        wq1, wq2 = Buf("wq1_d"), Buf("wq2_d")

        O = P.op

        def tap(name, src_ap, reads):
            if name in dbg:
                O("gpsimd", lambda e, s=src_ap, d=dbg[name]: e.dma_start(out=d, in_=s),
                  reads=list(reads), dma=True, out=True)

        O("vector", lambda e: e.memset(eps_sb[:], EPS), writes=[epsb])
        O("vector", lambda e: e.memset(onesall_sb[:], 1.0), writes=[onesb])
        O("sync", lambda e: e.dma_start(out=dmask_sb[:], in_=dmask_d), writes=[dmaskb], dma=True)
        O("sync", lambda e: e.dma_start(out=b1_sb[:], in_=b1_d), writes=[b1b], dma=True)
        O("sync", lambda e: e.dma_start(out=xb_sb[0:1, 0:1024], in_=b2_d), writes=[xb[0]], dma=True)
        for vsl in range(2):
            O("vector", lambda e, vsl=vsl: e.memset(
                vstage_sb[:, vsl * (H * 65):(vsl + 1) * (H * 65)]
                .rearrange("p (h c) -> p h c", h=H)[:, :, 64:65], 1.0),
              writes=[vstage[vsl]])
        for eh in range(2):
            O("tensor", lambda e, eh=eh: e.matmul(
                op_ps[:, 0:512], onesall_sb[0:1, :],
                xb_sb[0:1, eh * 512:(eh + 1) * 512],
                start=True, stop=True), reads=[onesb, xb[0]], writes=[op])
            O("vector", lambda e, eh=eh: e.tensor_copy(
                b2bc_sb[:, eh * 512:(eh + 1) * 512], op_ps[:, 0:512]),
              reads=[op], writes=[b2bc])

        def ln_tile(x_ap, out_ap, x_bufs, out_bufs, sl, act_apply=False):
            st = stats_sb[:, sl * 12:(sl + 1) * 12]
            ag = aggr_sb[:, sl * 2:(sl + 1) * 2]
            rs = rstd_sb[:, sl:sl + 1]
            O("vector", lambda e: e.bn_stats(st[:, 0:6], x_ap[:, 0:512]),
              reads=list(x_bufs), writes=[statsb[sl]])
            O("vector", lambda e: e.bn_stats(st[:, 6:12], x_ap[:, 512:1024]),
              reads=list(x_bufs), writes=[statsb[sl]])
            O("vector", lambda e: e.bn_aggr(ag, st),
              reads=[statsb[sl]], writes=[aggrb[sl]])
            O("scalar", lambda e: e.activation(rs, ag[:, 1:2], AF.Ln, bias=eps_sb[:, 0:1]),
              reads=[aggrb[sl], epsb], writes=[rstdb[sl]])
            O("scalar", lambda e: e.activation(rs, rs, AF.Exp, scale=-0.5),
              reads=[rstdb[sl]], writes=[rstdb[sl]])
            O("vector", lambda e: e.tensor_scalar(out_ap, x_ap, ag[:, 0:1], rs,
                                                  ALU.subtract, ALU.mult),
              reads=list(x_bufs) + [aggrb[sl], rstdb[sl]], writes=list(out_bufs))

        def load_w_steps(wd_ap, dst_sb, dst_buf):
            steps = []
            for step in range(16):
                c, hh = step // 2, step % 2
                sl = step % 2

                def emit(c=c, hh=hh, sl=sl):
                    qv = qscr_sb[:, sl * 512:(sl + 1) * 512]
                    O("sync", lambda e, c=c, hh=hh, qv=qv: e.dma_start(
                        out=qv, in_=wd_ap[c * 128:(c + 1) * 128,
                                          hh * 512:(hh + 1) * 512]),
                      writes=[qscr[sl]], dma=True)
                    O("scalar", lambda e, c=c, hh=hh, qv=qv: e.copy(
                        dst_sb[:, c * 1024 + hh * 512: c * 1024 + (hh + 1) * 512], qv),
                      reads=[qscr[sl]], writes=[dst_buf])
                steps.append(emit)
            return steps

        def load_w_bf16(wd_ap, dst_sb, dst_buf):
            for s_ in load_w_steps(wd_ap, dst_sb, dst_buf):
                s_()

        wq_steps = load_w_steps(wqT_d, wB_sb, wB)
        for g4 in range(4):
            O("sync", lambda e, g4=g4: e.dma_start(
                out=x_sb[:, g4 * 4096:(g4 + 1) * 4096]
                .rearrange("p (t d) -> p t d", t=4),
                in_=xo_d[g4 * 512:(g4 + 1) * 512, :]
                .rearrange("(t p) d -> p t d", p=128)),
              writes=x_t[g4 * 4:(g4 + 1) * 4], dma=True)
            for _ in range(4):
                if wq_steps:
                    wq_steps.pop(0)()
        def ln1b_tile(t):
            if True:
                sl = t % 2
                xa = xb_sb[:, sl * 1024:(sl + 1) * 1024]
                la = lno_sb[:, sl * 1024:(sl + 1) * 1024]
                O("sync", lambda e, t=t, xa=xa: e.dma_start(
                    out=xa, in_=xb_d[t * 128:(t + 1) * 128, :]), writes=[xb[sl]], dma=True)
                ln_tile(xa, la, [xb[sl]], [lno[sl]], sl)
                O("gpsimd", lambda e, t=t, la=la: e.dma_start(
                    out=ln1b_d[t * 128:(t + 1) * 128, :], in_=la),
                  reads=[lno[sl]], writes=[ln1b_g[t // 4]], dma=True)

        def ln1b_group(tt):
            for i in range(4):
                ln1b_tile(4 * tt + i)

        def kv_transposes(tt):
            fs = (tt % 2) * 8
            for dc in range(8):
                O("sync", lambda e, tt=tt, dc=dc, fs=fs: e.dma_start_transpose(
                    fm_sb[:, (fs + dc) * 512:(fs + dc + 1) * 512],
                    ln1b_d[tt * 512:(tt + 1) * 512, dc * 128:(dc + 1) * 128]),
                  reads=[ln1b_g[tt]], writes=[fm[fs + dc]], dma=True)

        load_w_bf16(wqT_d, wB_sb, wB)

        def ln1o_tile(t):
            sl = t % 2
            xa = x_sb[:, t * 1024:(t + 1) * 1024]
            la = lno_sb[:, sl * 1024:(sl + 1) * 1024]
            ln_tile(xa, la, [x_t[t]], [lno[sl]], sl)
            O("gpsimd", lambda e, t=t, la=la: e.dma_start(
                out=ln1o_d[t * 128:(t + 1) * 128, :], in_=la),
              reads=[lno[sl]], writes=[ln1o_g[t // 4]], dma=True)

        def ln1o_group(tt):
            for i in range(4):
                ln1o_tile(4 * tt + i)

        def q_transposes(tt):
            fs = (tt % 2) * 8
            for dc in range(8):
                O("sync", lambda e, tt=tt, dc=dc, fs=fs: e.dma_start_transpose(
                    fm_sb[:, (fs + dc) * 512:(fs + dc + 1) * 512],
                    ln1o_d[tt * 512:(tt + 1) * 512, dc * 128:(dc + 1) * 128]),
                  reads=[ln1o_g[tt]], writes=[fm[fs + dc]], dma=True)

        wk_steps = load_w_steps(wkT_d, wA_sb, wA)
        ln1o_group(0)
        q_transposes(0)
        if TT > 1:
            ln1o_group(1)
            q_transposes(1)
        for tt in range(TT):
            fs = (tt % 2) * 8
            for qc in range(8):
                if qc % 2 == 0:
                    if tt + 2 < TT:
                        ln1o_tile(4 * (tt + 2) + qc // 2)
                    ln1b_tile(4 * tt + qc // 2)
                pslot = qc % 2
                pa = sc_ps[:, pslot * 512:(pslot + 1) * 512]
                for dc in range(8):
                    O("tensor", lambda e, qc=qc, dc=dc, fs=fs, pa=pa: e.matmul(
                        pa, wB_sb[:, dc * 1024 + qc * 128: dc * 1024 + (qc + 1) * 128],
                        fm_sb[:, (fs + dc) * 512:(fs + dc + 1) * 512],
                        start=(dc == 0), stop=(dc == 7)),
                      reads=[wB, fm[fs + dc]], writes=[sc[pslot]])
                O("scalar", lambda e, qc=qc, tt=tt, pa=pa: e.copy(
                    qu_sb[:, qc * 2048 + tt * 512: qc * 2048 + (tt + 1) * 512], pa),
                  reads=[sc[pslot]], writes=[qu])
            if tt + 2 < TT:
                q_transposes(tt + 2)
            for _ in range(4):
                if wk_steps:
                    wk_steps.pop(0)()
        while wk_steps:
            wk_steps.pop(0)()
        q_transposes(0)
        for tt in range(TT):
            fs = (tt % 2) * 8
            if tt + 1 < TT:
                q_transposes(tt + 1)
            for qc in range(8):
                pslot = qc % 2
                pa = sc_ps[:, pslot * 512:(pslot + 1) * 512]
                for dc in range(8):
                    O("tensor", lambda e, qc=qc, dc=dc, fs=fs, pa=pa: e.matmul(
                        pa, wB_sb[:, dc * 1024 + qc * 128: dc * 1024 + (qc + 1) * 128],
                        fm_sb[:, (fs + dc) * 512:(fs + dc + 1) * 512],
                        start=(dc == 0), stop=(dc == 7)),
                      reads=[wB, fm[fs + dc]], writes=[sc[pslot]])
                O("scalar", lambda e, qc=qc, tt=tt, pa=pa: e.copy(
                    qu_sb[:, qc * 2048 + tt * 512: qc * 2048 + (tt + 1) * 512], pa),
                  reads=[sc[pslot]], writes=[qu])

        quant_items = []
        NQ = 2

        def q_abs(c, cc, wd_ap, col, idx):
            sl = idx % NQ
            qv = qscr_sb[:, sl * 512:(sl + 1) * 512]

            def load():
                O("scalar", lambda e, qv=qv: e.dma_start(
                    out=qv, in_=wd_ap[c * 128:(c + 1) * 128, cc * 512:(cc + 1) * 512]),
                  writes=[qscr[sl]], dma=True)

            def compute():
                O("vector", lambda e, qv=qv: e.reduce_sum(
                    asum_sb[:, col:col + 1], qv, axis=AX.X, apply_absolute_value=True),
                  reads=[qscr[sl]], writes=[asum])
            return load, compute

        qidx = 0
        for c in range(8):
            for cc in range(8):
                quant_items.append(q_abs(c, cc, w1T_d, qidx, qidx)); qidx += 1
        for c in range(32):
            for cc in range(2):
                quant_items.append(q_abs(c, cc, w2T_d, qidx, qidx)); qidx += 1

        def q_gamma():
            O("tensor", lambda e: e.matmul(op_ps[0:1, 0:128], onesall_sb[:, 0:1],
                                           asum_sb[:, 0:128], start=True, stop=True),
              reads=[onesb, asum], writes=[op])
            O("vector", lambda e: e.reduce_sum(gam_sb[0:1, 0:1], op_ps[0:1, 0:64],
                                               axis=AX.X),
              reads=[op], writes=[gam])
            O("vector", lambda e: e.reduce_sum(gam_sb[0:1, 1:2], op_ps[0:1, 64:128],
                                               axis=AX.X),
              reads=[op], writes=[gam])
            O("vector", lambda e: e.tensor_single_scalar(
                gam_sb[0:1, 0:2], gam_sb[0:1, 0:2], float(NW) * EPS, ALU.add),
              reads=[gam], writes=[gam])
            O("vector", lambda e: e.reciprocal(gam_sb[0:1, 2:4], gam_sb[0:1, 0:2]),
              reads=[gam], writes=[gam])
            O("vector", lambda e: e.tensor_single_scalar(
                gam_sb[0:1, 2:4], gam_sb[0:1, 2:4], float(NW), ALU.mult),
              reads=[gam], writes=[gam])
            for w in range(2):
                O("tensor", lambda e, w=w: e.matmul(
                    op_ps[:, 128 + w:129 + w], onesall_sb[0:1, :],
                    gam_sb[0:1, 2 + w:3 + w], start=True, stop=True),
                  reads=[onesb, gam], writes=[op])
                O("vector", lambda e, w=w: e.tensor_copy(
                    ginv_sb[:, w:w + 1], op_ps[:, 128 + w:129 + w]),
                  reads=[op], writes=[ginv])

        quant_items.append((lambda: None, q_gamma))

        def q_quant_common(qv, pv, w, store_fn, sl):
            O("vector", lambda e, qv=qv, pv=pv: e.tensor_scalar(
                pv, qv, ginv_sb[:, w:w + 1], 0.5, ALU.mult, ALU.is_gt),
              reads=[qscr[sl], ginv], writes=[qp[sl]])
            nv = qn_sb[:, 0:512]
            O("vector", lambda e, qv=qv, nv=nv: e.tensor_scalar(
                nv, qv, ginv_sb[:, w:w + 1], -0.5, ALU.mult, ALU.is_lt),
              reads=[qscr[sl], ginv], writes=[qn[0]])
            O("vector", lambda e, pv=pv, nv=nv: e.tensor_sub(pv, pv, nv),
              reads=[qp[sl], qn[0]], writes=[qp[sl]])
            store_fn(pv, sl)

        def q_quant_w1(c, cc, idx):
            sl = idx % NQ
            qv = qscr_sb[:, sl * 512:(sl + 1) * 512]
            pv = qp_sb[:, sl * 512:(sl + 1) * 512]

            def load():
                O("scalar", lambda e, qv=qv: e.dma_start(
                    out=qv, in_=w1T_d[c * 128:(c + 1) * 128, cc * 512:(cc + 1) * 512]),
                  writes=[qscr[sl]], dma=True)

            def store(pv, sl):
                O("gpsimd", lambda e, pv=pv, c=c, cc=cc: e.dma_start(
                    out=wq1_d[cc * 4:(cc + 1) * 4, :, c, :].rearrange(
                        "fc p f -> p fc f"),
                    in_=pv.rearrange("p (fc f) -> p fc f", fc=4)),
                  reads=[qp[sl]], writes=[wq1], dma=True)

            return load, (lambda: q_quant_common(qv, pv, 0, store, sl))

        def q_quant_w2(c, cc, idx):
            sl = idx % NQ
            qv = qscr_sb[:, sl * 512:(sl + 1) * 512]
            pv = qp_sb[:, sl * 512:(sl + 1) * 512]

            def load():
                O("scalar", lambda e, qv=qv: e.dma_start(
                    out=qv, in_=w2T_d[c * 128:(c + 1) * 128, cc * 512:(cc + 1) * 512]),
                  writes=[qscr[sl]], dma=True)

            def store(pv, sl):
                O("gpsimd", lambda e, pv=pv, c=c, cc=cc: e.dma_start(
                    out=wq2_d[c * 128:(c + 1) * 128, cc * 512:(cc + 1) * 512], in_=pv),
                  reads=[qp[sl]], writes=[wq2], dma=True)

            return load, (lambda: q_quant_common(qv, pv, 1, store, sl))

        for c in range(8):
            for cc in range(8):
                quant_items.append(q_quant_w1(c, cc, qidx)); qidx += 1
        for c in range(32):
            for cc in range(2):
                quant_items.append(q_quant_w2(c, cc, qidx)); qidx += 1

        q_load_next = [0]
        q_comp_next = [0]
        QLEAD = NQ

        def quant_step(n):
            for _ in range(n):
                if (q_load_next[0] - q_comp_next[0] >= QLEAD
                        or (q_load_next[0] >= len(quant_items)
                            and q_comp_next[0] < len(quant_items))):
                    quant_items[q_comp_next[0]][1]()
                    q_comp_next[0] += 1
                if q_load_next[0] < len(quant_items):
                    quant_items[q_load_next[0]][0]()
                    q_load_next[0] += 1

        wv_steps = load_w_steps(wvT_d, wB_sb, wB)
        vs_ct = 0

        kv_transposes(0)
        kv_transposes(1)
        for tt in range(S // 512):
            fs = (tt % 2) * 8
            for kc in range(8):
                if wv_steps and kc % 2 == 1:
                    for _ in range(4):
                        if wv_steps:
                            wv_steps.pop(0)()
                if kc % 2 == 0 and tt + 4 < S // 512:
                    ln1b_tile(4 * (tt + 4) + kc // 2)
                pslot = kc % 4
                pa = sc_ps[:, pslot * 512:(pslot + 1) * 512]
                ke = kevac_sb[:, (kc % 2) * 512:(kc % 2 + 1) * 512]
                for dc in range(8):
                    O("tensor", lambda e, kc=kc, dc=dc, fs=fs, pa=pa: e.matmul(
                        pa, wA_sb[:, dc * 1024 + kc * 128: dc * 1024 + (kc + 1) * 128],
                        fm_sb[:, (fs + dc) * 512:(fs + dc + 1) * 512],
                        start=(dc == 0), stop=(dc == 7)),
                      reads=[wA, fm[fs + dc]], writes=[sc[pslot]])
                O("scalar", lambda e, ke=ke, pa=pa: e.copy(ke, pa),
                  reads=[sc[pslot]], writes=[kevac[kc % 2]])
                O("gpsimd", lambda e, kc=kc, tt=tt, ke=ke: e.dma_start(
                    out=kfm_d[kc * 128:(kc + 1) * 128, tt * 512:(tt + 1) * 512],
                    in_=ke),
                  reads=[kevac[kc % 2]], writes=[kfm], dma=True)
            for ts in range(4):
                vsl = vs_ct % 2
                vs_ct += 1
                va = vstage_sb[:, vsl * (H * 65):(vsl + 1) * (H * 65)]
                for vh in range(2):
                    pa = oa_ps[:, vh * 512:(vh + 1) * 512]
                    for dc in range(8):
                        O("tensor", lambda e, ts=ts, vh=vh, dc=dc, fs=fs, pa=pa: e.matmul(
                            pa, fm_sb[:, (fs + dc) * 512 + ts * 128:
                                      (fs + dc) * 512 + (ts + 1) * 128],
                            wB_sb[:, dc * 1024 + vh * 512: dc * 1024 + (vh + 1) * 512],
                            start=(dc == 0), stop=(dc == 7)),
                          reads=[wB, fm[fs + dc]], writes=[oa[vh]])
                    O("scalar", lambda e, vh=vh, va=va, pa=pa: e.copy(
                        va.rearrange("p (h c) -> p h c", h=H)[:, vh * 8:(vh + 1) * 8, 0:64],
                        pa.rearrange("p (h c) -> p h c", h=8)),
                      reads=[oa[vh]], writes=[vstage[vsl]])
                O("gpsimd", lambda e, tt=tt, ts=ts, va=va: e.dma_start(
                    out=vtm_d[tt * 512 + ts * 128: tt * 512 + (ts + 1) * 128, :],
                    in_=va[:, 0:H * 65] if False else va),
                  reads=[vstage[vsl]], writes=[vtm], dma=True)
            if tt + 2 < S // 512:
                kv_transposes(tt + 2)
        tap("ln1b", ln1b_d, ln1b_g)
        tap("kfm", kfm_d, [kfm])
        tap("vtm", vtm_d, [vtm])

        wo_steps = load_w_steps(woT_d, wA_sb, wA)

        ln3_q = []
        O("vector", lambda e: e.memset(op_ps[:, 0:1024], 0.0), writes=[op])
        gb_ct = 0

        def band_load(T, hp, b):
            nonlocal gb_ct
            sl = gb_ct % 2
            gb_ct += 1
            O("sync", lambda e, hp=hp, b=b, sl=sl: e.dma_start(
                out=kband_sb[:, sl * 1024:(sl + 1) * 1024],
                in_=kfm_d[hp * 128:(hp + 1) * 128, b * 1024:(b + 1) * 1024]),
              reads=[kfm], writes=[kband[sl]], dma=True)
            O("sync", lambda e, hp=hp, b=b, sl=sl: e.dma_start(
                out=vband_sb[:, sl * 1040:(sl + 1) * 1040]
                .rearrange("p (kt c) -> p kt c", kt=8),
                in_=vtm_d[b * 1024:(b + 1) * 1024, hp * 130:(hp + 1) * 130]
                .rearrange("(kt p) c -> p kt c", p=128)),
              reads=[vtm], writes=[vband[sl]], dma=True)
            return sl

        for T in range(TT):
            kvr = 2048 * (T + 1)
            nb = kvr // 1024
            nkb = kvr // 128
            blocks = [(hp, b) for hp in range(8) for b in range(nb)]
            slot_of = {}
            slot_of[blocks[0]] = band_load(T, *blocks[0])
            if len(blocks) > 1:
                slot_of[blocks[1]] = band_load(T, *blocks[1])
            pairs = [
                (sc_ps[:, 0:1024], [sc[0], sc[1]], expT_sb[:, 0:1024],
                 [expT[0], expT[1]]),
                (sc_ps[:, 1024:2048], [sc[2], sc[3]], expT_sb[:, 1024:2048],
                 [expT[2], expT[3]]),
                (op_ps[:, 0:1024], [op], kevac_sb[:, 0:1024],
                 [kevac[0], kevac[1]]),
            ]
            av_q = []
            uc = [0]
            epi_q = []

            def flush_av(n):
                while len(av_q) > n:
                    av_q.pop(0)()

            for bi, (hp, b) in enumerate(blocks):
                sl = slot_of.pop((hp, b))
                for g8 in range(8):
                    kb = 8 * b + g8
                    j = kb - 16 * T
                    qoff = 32 * j if j >= 0 else 0
                    pp, pbufs, ee, ebufs = pairs[uc[0] % 3]
                    uc[0] += 1
                    for h2 in range(2):
                        O("tensor", lambda e, h2=h2, hp=hp, kb=kb, T=T,
                          sl=sl, qoff=qoff, pp=pp: e.matmul(
                            pp[:, h2 * 512 + qoff:(h2 + 1) * 512],
                            kband_sb[64 * h2: 64 * h2 + 64,
                                     sl * 1024 + (kb % 8) * 128:
                                     sl * 1024 + (kb % 8 + 1) * 128],
                            qu_sb[64 * h2: 64 * h2 + 64,
                                  hp * 2048 + T * 512 + qoff:
                                  hp * 2048 + (T + 1) * 512],
                            start=True, stop=True),
                          reads=[kband[sl], qu], writes=pbufs)
                    flush_av(2)
                    if j >= 3:
                        for h2 in range(2):
                            O("scalar", lambda e, h2=h2, qoff=qoff, pp=pp,
                              ee=ee: e.activation(
                                ee[:, h2 * 512 + qoff:(h2 + 1) * 512],
                                pp[:, h2 * 512 + qoff:(h2 + 1) * 512],
                                AF.Exp, scale=0.125),
                              reads=pbufs, writes=ebufs)
                    else:
                        O("scalar", lambda e, pp=pp, ee=ee: e.activation(
                            ee, pp, AF.Exp, scale=0.125),
                          reads=pbufs, writes=ebufs)
                    if j >= 0:
                        for h2 in range(2):
                            O("gpsimd", lambda e, h2=h2, qoff=qoff, ee=ee: e.tensor_mul(
                                ee[:, h2 * 512 + qoff: h2 * 512 + qoff + 32],
                                ee[:, h2 * 512 + qoff: h2 * 512 + qoff + 32],
                                dmask_sb[:]),
                              reads=ebufs + [dmaskb], writes=ebufs)
                    def make_av(kb=kb, sl=sl, qoff=qoff, nkb=nkb, ee=ee,
                                ebufs=ebufs):
                        def emit_av():
                            for h2 in range(2):
                                O("tensor", lambda e, h2=h2, kb=kb, sl=sl,
                                  qoff=qoff, st=(kb == 0), ee=ee,
                                  sp=(kb == nkb - 1): e.matmul(
                                    oa_ps[0:65, h2 * 512 + qoff:(h2 + 1) * 512],
                                    vband_sb[:, sl * 1040 + (kb % 8) * 130 + h2 * 65:
                                             sl * 1040 + (kb % 8) * 130 + (h2 + 1) * 65],
                                    ee[:, h2 * 512 + qoff:(h2 + 1) * 512],
                                    start=st, stop=sp),
                                  reads=[vband[sl]] + ebufs, writes=[oa[h2]])
                        return emit_av

                    av_q.append(make_av())
                if bi + 2 < len(blocks):
                    flush_av(0)
                    slot_of[blocks[bi + 2]] = band_load(T, *blocks[bi + 2])
                if epi_q:
                    epi_q.pop(0)()
                if wo_steps:
                    wo_steps.pop(0)()
                    if wo_steps:
                        wo_steps.pop(0)()
                else:
                    quant_step(2)
                if ln3_q:
                    ln3_q.pop(0)()

                if b == nb - 1:
                    flush_av(0)
                    O("vector", lambda e: e.tensor_copy(
                        zscr_sb[0:65, 0:1024], oa_ps[0:65, 0:1024]),
                      reads=[oa[0], oa[1]], writes=[zscr, onf])

                    def make_epi(hp=hp):
                        def emit_epi():
                            O("vector", lambda e: e.reciprocal(
                                zscr_sb[64:65, 0:1024], zscr_sb[64:65, 0:1024]),
                              reads=[zscr], writes=[zscr])
                            for h2 in range(2):
                                hs = h2 * 512
                                O("tensor", lambda e, hs=hs: e.matmul(
                                    op_ps[0:64, hs:hs + 512],
                                    onesall_sb[64:65, 0:64],
                                    zscr_sb[64:65, hs:hs + 512],
                                    start=True, stop=True),
                                  reads=[onesb, zscr], writes=[op])
                            O("vector", lambda e, hp=hp: e.tensor_mul(
                                onorm_sb[0:64, hp * 512:(hp + 1) * 512],
                                onf_sb[0:64, 0:512], op_ps[0:64, 0:512]),
                              reads=[onf, op], writes=[onorm[hp]])
                            O("vector", lambda e: e.tensor_mul(
                                ontmp_sb[0:64, 0:512], onf_sb[0:64, 512:1024],
                                op_ps[0:64, 512:1024]),
                              reads=[onf, op], writes=[w1t[0]])
                            O("gpsimd", lambda e, hp=hp: e.dma_start(
                                out=onorm_sb[64:128, hp * 512:(hp + 1) * 512],
                                in_=ontmp_sb[0:64, 0:512]),
                              reads=[w1t[0]], writes=[onorm[hp]], dma=True)
                        return emit_epi

                    epi_q.append(make_epi())
            while epi_q:
                epi_q.pop(0)()
            for qs in range(4):
                xt = 4 * T + qs
                xa = x_sb[:, xt * 1024:(xt + 1) * 1024]
                if qs % 2 == 0:
                    pfull, pwb = op_ps[:, 0:1024], [op]
                else:
                    pfull, pwb = sc_ps[:, 0:1024], [sc[0], sc[1]]
                for pr in range(8):
                    for eh in range(2):
                        O("tensor", lambda e, qs=qs, eh=eh, pr=pr, pfull=pfull: e.matmul(
                            pfull[:, eh * 512:(eh + 1) * 512],
                            onorm_sb[:, pr * 512 + qs * 128: pr * 512 + (qs + 1) * 128],
                            wA_sb[:, pr * 1024 + eh * 512: pr * 1024 + (eh + 1) * 512],
                            start=(pr == 0), stop=(pr == 7)),
                          reads=[onorm[pr], wA], writes=pwb)
                sl = qs % 2
                st = stats_sb[:, sl * 12:(sl + 1) * 12]
                ag = aggr_sb[:, sl * 2:(sl + 1) * 2]
                rs = rstd_sb[:, sl:sl + 1]
                O("vector", lambda e, pfull=pfull, st=st: e.bn_stats(
                    st[:, 0:6], pfull[:, 0:512]), reads=pwb, writes=[statsb[sl]])
                O("vector", lambda e, pfull=pfull, st=st: e.bn_stats(
                    st[:, 6:12], pfull[:, 512:1024]), reads=pwb, writes=[statsb[sl]])
                O("vector", lambda e, st=st, ag=ag: e.bn_aggr(ag, st),
                  reads=[statsb[sl]], writes=[aggrb[sl]])
                O("scalar", lambda e, ag=ag, rs=rs: e.activation(
                    rs, ag[:, 1:2], AF.Ln, bias=eps_sb[:, 0:1]),
                  reads=[aggrb[sl], epsb], writes=[rstdb[sl]])
                O("scalar", lambda e, rs=rs: e.activation(rs, rs, AF.Exp, scale=-0.5),
                  reads=[rstdb[sl]], writes=[rstdb[sl]])
                O("vector", lambda e, pfull=pfull, ag=ag, rs=rs: e.tensor_scalar(
                    lntmp_sb[:, 0:1024], pfull, ag[:, 0:1], rs, ALU.subtract, ALU.mult),
                  reads=pwb + [aggrb[sl], rstdb[sl]], writes=[lntmpb])
                O("gpsimd", lambda e, xa=xa: e.tensor_add(xa, xa, lntmp_sb[:, 0:1024]),
                  reads=[x_t[xt], lntmpb], writes=[x_t[xt]])
                def make_ln3(xt=xt, sl=sl, T=T):
                    def emit_ln3():
                        xa2 = x_sb[:, xt * 1024:(xt + 1) * 1024]
                        la = lno_sb[:, sl * 1024:(sl + 1) * 1024]
                        ln_tile(xa2, la, [x_t[xt]], [lno[sl]], sl)
                        O("gpsimd", lambda e, xt=xt, la=la: e.dma_start(
                            out=ln3_d[xt * 128:(xt + 1) * 128, :], in_=la),
                          reads=[lno[sl]], writes=[ln3_g[T]], dma=True)
                    return emit_ln3

                ln3_q.append(make_ln3())
        while ln3_q:
            ln3_q.pop(0)()
        tap("ln3", ln3_d, ln3_g)
        while q_comp_next[0] < len(quant_items):
            quant_step(4)
        tap("wq1", wq1_d, [wq1])
        tap("wq2", wq2_d, [wq2])

        def ffn_transposes(tt):
            fs = (tt % 2) * 8
            for dc in range(8):
                O("sync", lambda e, tt=tt, dc=dc, fs=fs: e.dma_start_transpose(
                    fm_sb[:, (fs + dc) * 512:(fs + dc + 1) * 512],
                    ln3_d[tt * 512:(tt + 1) * 512, dc * 128:(dc + 1) * 128]),
                  reads=[ln3_g[tt]], writes=[fm[fs + dc]], dma=True)

        ffn_transposes(0)
        ffn_q = []
        w1s = [(w1t_sb[:, 0:1024], w1t[0]), (w1t_sb[:, 1024:2048], w1t[1]),
               (kband_sb[:, 0:1024], kband[0]), (kband_sb[:, 1024:2048], kband[1])]
        w2s = [(w2t_sb[:, 0:1024], w2t[0]), (w2t_sb[:, 1024:2048], w2t[1]),
               (vband_sb[:, 0:1024], vstage[0]), (vband_sb[:, 1040:2064], vstage[1])]
        for tt in range(TT):
            fs = (tt % 2) * 8
            if tt + 1 < TT:
                ffn_transposes(tt + 1)
            for fc in range(32):
                if fc == 0:
                    while len(ffn_q) > 2:
                        ffn_q.pop(0)()
                elif fc in (8, 16) and ffn_q:
                    ffn_q.pop(0)()
                wv, wb = w1s[fc % 4]
                O("sync", lambda e, fc=fc, wv=wv: e.dma_start(
                    out=wv, in_=wq1_d[fc, :, :, :].rearrange("p c f -> p (c f)")),
                  reads=[wq1], writes=[wb], dma=True)
                ua = oa_ps[:, (fc % 2) * 512:(fc % 2 + 1) * 512]
                for dc in range(8):
                    O("tensor", lambda e, fc=fc, dc=dc, fs=fs, wv=wv, ua=ua: e.matmul(
                        ua, wv[:, dc * 128:(dc + 1) * 128],
                        fm_sb[:, (fs + dc) * 512:(fs + dc + 1) * 512],
                        start=(dc == 0), stop=(dc == 7)),
                      reads=[wb, fm[fs + dc]], writes=[oa[fc % 2]])
                O("vector", lambda e, fc=fc, ua=ua: e.tensor_scalar(
                    qu_sb[:, fc * 512:(fc + 1) * 512], ua,
                    b1_sb[:, fc:fc + 1], 0.0, ALU.add, ALU.max),
                  reads=[oa[fc % 2], b1b], writes=[qu])
            pmap = {(0, 0): (sc_ps[:, 0:512], sc[0]), (0, 1): (sc_ps[:, 512:1024], sc[1]),
                    (0, 2): (sc_ps[:, 1024:1536], sc[2]), (0, 3): (sc_ps[:, 1536:2048], sc[3]),
                    (1, 0): (oa_ps[:, 0:512], oa[0]), (1, 1): (oa_ps[:, 512:1024], oa[1]),
                    (1, 2): (op_ps[:, 0:512], op), (1, 3): (op_ps[:, 512:1024], op)}
            for fc in range(32):
                wv, wb = w2s[fc % 4]
                O("sync", lambda e, fc=fc, wv=wv: e.dma_start(
                    out=wv, in_=wq2_d[fc * 128:(fc + 1) * 128, :]),
                  reads=[wq2], writes=[wb], dma=True)
                for ts in range(4):
                    for eh in range(2):
                        pa, pb = pmap[(eh, ts)]
                        O("tensor", lambda e, fc=fc, eh=eh, ts=ts, pa=pa, wv=wv: e.matmul(
                            pa, qu_sb[:, fc * 512 + ts * 128: fc * 512 + (ts + 1) * 128],
                            wv[:, eh * 512:(eh + 1) * 512],
                            start=(fc == 0), stop=(fc == 31)),
                          reads=[qu, wb], writes=[pb])
            for ts in range(4):
                def make_ln4(tt=tt, ts=ts):
                    def emit_ln4():
                        xt = 4 * tt + ts
                        xa = x_sb[:, xt * 1024:(xt + 1) * 1024]
                        ff = ffno_sb[:, 1024:2048]
                        for eh in (1, 0):
                            pa, pb = pmap[(eh, ts)]
                            O("vector", lambda e, pa=pa, eh=eh, ff=ff: e.tensor_add(
                                ff[:, eh * 512:(eh + 1) * 512], pa,
                                b2bc_sb[:, eh * 512:(eh + 1) * 512]),
                              reads=[pb, b2bc], writes=[xb[1]])
                        sl = ts % 2
                        st = stats_sb[:, sl * 12:(sl + 1) * 12]
                        ag = aggr_sb[:, sl * 2:(sl + 1) * 2]
                        rs = rstd_sb[:, sl:sl + 1]
                        O("vector", lambda e, ff=ff, st=st: e.bn_stats(
                            st[:, 0:6], ff[:, 0:512]),
                          reads=[xb[1]], writes=[statsb[sl]])
                        O("vector", lambda e, ff=ff, st=st: e.bn_stats(
                            st[:, 6:12], ff[:, 512:1024]),
                          reads=[xb[1]], writes=[statsb[sl]])
                        O("vector", lambda e, st=st, ag=ag: e.bn_aggr(ag, st),
                          reads=[statsb[sl]], writes=[aggrb[sl]])
                        O("scalar", lambda e, ag=ag, rs=rs: e.activation(
                            rs, ag[:, 1:2], AF.Ln, bias=eps_sb[:, 0:1]),
                          reads=[aggrb[sl], epsb], writes=[rstdb[sl]])
                        O("scalar", lambda e, rs=rs: e.activation(
                            rs, rs, AF.Exp, scale=-0.5),
                          reads=[rstdb[sl]], writes=[rstdb[sl]])
                        O("vector", lambda e, ff=ff, ag=ag, rs=rs: e.tensor_scalar(
                            lntmp_sb[:, 0:1024], ff, ag[:, 0:1], rs, ALU.subtract, ALU.mult),
                          reads=[xb[1], aggrb[sl], rstdb[sl]], writes=[lntmpb])
                        O("gpsimd", lambda e, xa=xa: e.tensor_add(xa, xa, lntmp_sb[:, 0:1024]),
                          reads=[x_t[xt], lntmpb], writes=[x_t[xt]])
                    return emit_ln4

                ffn_q.append(make_ln4())
        while ffn_q:
            ffn_q.pop(0)()

        O("sync", lambda e: e.dma_start(
            out=out_d.rearrange("(t p) d -> p t d", p=128),
            in_=x_sb[:].rearrange("p (t d) -> p t d", t=NT)),
          reads=x_t, dma=True, out=True)

        with contextlib.ExitStack() as es2:
            sems = {e: es2.enter_context(nc.semaphore("s_" + e)) for e in Prog.ENGS}
            dsems = {}
            for q in Prog.DMAQ:
                for s in range(Prog.R):
                    dsems[(q, s)] = es2.enter_context(nc.semaphore(f"d_{q}{s}"))
            with nc.Block() as block:
                P.emit(block, sems, dsems)

    return nc


def _stage_inputs(inputs):
    x = np.asarray(inputs["L0_attn_x"], dtype=np.float32)
    wq = np.asarray(inputs["L0_attn_q_heads"], dtype=np.float32)
    wk = np.asarray(inputs["L0_attn_k_heads"], dtype=np.float32)
    wv = np.asarray(inputs["L0_attn_v_heads"], dtype=np.float32)
    wo = np.asarray(inputs["L0_attn_output_proj"], dtype=np.float32)
    w1 = np.asarray(inputs["L0_ffn_W1"], dtype=np.float32)
    b1 = np.asarray(inputs["L0_ffn_b1"], dtype=np.float32)
    w2 = np.asarray(inputs["L0_ffn_W2"], dtype=np.float32)
    b2 = np.asarray(inputs["L0_ffn_b2"], dtype=np.float32)

    wqT = np.ascontiguousarray(wq.T)
    wkT = np.ascontiguousarray(wk.T)
    wvT = np.ascontiguousarray(wv.T)
    woT = np.ascontiguousarray(wo.T)
    w1T = np.ascontiguousarray(w1.T)
    w2T = np.ascontiguousarray(w2.T)
    b2r = np.ascontiguousarray(b2.reshape(1, D))
    b1r = np.ascontiguousarray(b1.reshape(FF // 128, 128).T)

    in_maps = []
    for c in range(8):
        b = c // 4
        r = c % 4
        kk, ttau = np.meshgrid(np.arange(128), np.arange(32), indexing="ij")
        dmask = (kk <= 4 * ttau + r).astype(ml_dtypes.bfloat16)
        xb = np.ascontiguousarray(x[b])
        xo = np.ascontiguousarray(x[b, r::4])
        in_maps.append({
            "x_batch": xb, "x_own": xo,
            "wqT": wqT, "wkT": wkT, "wvT": wvT, "woT": woT,
            "w1T": w1T, "w2T": w2T, "b1": b1r, "b2": b2r,
            "dmask": np.ascontiguousarray(dmask),
        })
    return in_maps


def kernel(**inputs):
    in_maps = _stage_inputs(inputs)
    nc = build_program()
    res = run_bass_kernel_spmd(nc, in_maps, core_ids=list(range(8)))
    out = np.empty((B, S, D), dtype=np.float32)
    for c in range(8):
        b, r = c // 4, c % 4
        out[b, r::4, :] = res.results[c]["out"]
    return out
